# version 13
# baseline (speedup 1.0000x reference)
import contextlib
import math

import numpy as np

import concourse.bass as bass
import concourse.mybir as mybir
from concourse.bass_utils import run_bass_kernel_spmd

F32 = mybir.dt.float32
AF = mybir.ActivationFunctionType
ALU = mybir.AluOpType
AX = mybir.AxisListType

MASKVAL = -30000.0
EPS = 1e-5
LAM_INIT = float(0.8 - 0.6 * math.exp(-0.3 * 0.0))
OUT_FACTOR = 1.0 - LAM_INIT
ENGS = ("sync", "tensor", "vector", "scalar", "gpsimd")


def make_cfg(B=4, T=2048, D=2048, H=16, KV=4, n_cores=8):
    c = {}
    c["B"], c["T"], c["D"], c["H"], c["KV"] = B, T, D, H, KV
    c["TOWN"] = T // 2
    c["NT"] = T // 128
    c["NO"] = c["TOWN"] // 128
    c["DC"] = D // 128
    c["HQ"] = H * 256
    c["KC"] = KV * 256
    c["QC"] = c["HQ"] // 128
    c["KFC"] = c["KC"] // 128
    c["n_cores"] = n_cores
    c["scale"] = 1.0 / math.sqrt(128.0)
    return c



def _deinterleave_perm(ncols, block=128):
    p = []
    for b0 in range(0, ncols, block):
        p += [b0 + 2 * j for j in range(block // 2)]
        p += [b0 + 2 * j + 1 for j in range(block // 2)]
    return np.array(p, dtype=np.int64)


def _chunk_cols(mat, pdim=128):
    R, C = mat.shape
    return np.ascontiguousarray(
        mat.reshape(R // pdim, pdim, C).transpose(1, 0, 2).reshape(pdim, (R // pdim) * C)
    )


def make_core_inputs(cfg, inputs):
    B, T, D, H, KV = cfg["B"], cfg["T"], cfg["D"], cfg["H"], cfg["KV"]
    NT, NO, DC, HQ, KC = cfg["NT"], cfg["NO"], cfg["DC"], cfg["HQ"], cfg["KC"]
    TOWN = cfg["TOWN"]
    f32 = np.float32

    x = np.asarray(inputs["L0_attn_x"], dtype=f32)
    wq = np.asarray(inputs["L0_attn_Wq"], dtype=f32)
    wk = np.asarray(inputs["L0_attn_Wk"], dtype=f32)
    wv = np.asarray(inputs["L0_attn_Wv"], dtype=f32)
    wo = np.asarray(inputs["L0_attn_Wo"], dtype=f32)
    gn_w = np.asarray(inputs["L0_attn_gn_w"], dtype=f32)
    gn_b = np.asarray(inputs["L0_attn_gn_b"], dtype=f32)
    rw = np.asarray(inputs["L0_ffn_router_w"], dtype=f32)
    dw = np.asarray(inputs["L0_ffn_down_w"], dtype=f32)
    uw = np.asarray(inputs["L0_ffn_up_w"], dtype=f32)

    permq = _deinterleave_perm(HQ)
    permk = _deinterleave_perm(KC)
    wq_mv = _chunk_cols(wq[:, permq])
    wk_st = _chunk_cols(wk[:, permk])
    wv_sb = _chunk_cols(wv)
    wo_st = _chunk_cols(wo)
    dw_sbh = _chunk_cols(np.ascontiguousarray(dw.reshape(128, D).T))
    up_sbh = np.ascontiguousarray(uw.transpose(0, 2, 1).reshape(128, D))
    rt_sbh = _chunk_cols(np.ascontiguousarray(rw.T))
    gnw2 = np.ascontiguousarray(gn_w.reshape(H, 2, 128).transpose(2, 1, 0).reshape(128, 2 * H))
    gnb2 = np.ascontiguousarray(gn_b.reshape(H, 2, 128).transpose(2, 1, 0).reshape(128, 2 * H))

    lq1 = np.ascontiguousarray(np.asarray(inputs["L0_attn_lambda_q1"], f32).reshape(1, 128))
    lk1 = np.ascontiguousarray(np.asarray(inputs["L0_attn_lambda_k1"], f32).reshape(1, 128))
    lq2 = np.ascontiguousarray(np.asarray(inputs["L0_attn_lambda_q2"], f32).reshape(1, 128))
    lk2 = np.ascontiguousarray(np.asarray(inputs["L0_attn_lambda_k2"], f32).reshape(1, 128))

    ident = np.eye(128, dtype=f32)
    ones_col = np.ones((128, 1), dtype=f32)
    ones_row = np.ones((1, 128), dtype=f32)
    rep8 = np.zeros((8, 128), dtype=f32)
    for e in range(8):
        rep8[e, e * 16:(e + 1) * 16] = 1.0

    inv = 1.0 / (10000.0 ** (np.arange(0, 128, 2, dtype=np.float64) / 128.0))
    tpos = np.arange(T, dtype=np.float64)
    ang = tpos[:, None] * inv[None, :]
    cosk = np.ascontiguousarray(
        np.concatenate([np.cos(ang).T, np.sin(ang).T], axis=0).astype(f32))

    tri = np.where(np.arange(128)[None, :] > np.arange(128)[:, None], MASKVAL, 0.0).astype(f32)

    maps = []
    for core in range(cfg["n_cores"]):
        b = core // 2
        par = core % 2
        own_tiles = [2 * p + par for p in range(NO)]
        xb = x[b]
        x_all = np.ascontiguousarray(xb)
        x_own = np.ascontiguousarray(xb.reshape(NT, 128, D)[own_tiles].reshape(TOWN, D))
        own_pos = np.concatenate([np.arange(t0 * 128, t0 * 128 + 128) for t0 in own_tiles])
        a2 = own_pos.astype(np.float64)[:, None] * inv[None, :]
        cq = np.ascontiguousarray(
            np.concatenate([np.cos(a2).T, np.sin(a2).T], axis=0).astype(f32))
        maskt = np.zeros((128, 256), dtype=f32)
        if par == 1:
            maskt[:, 128:] = tri
        else:
            maskt[:, :128] = tri
            maskt[:, 128:] = MASKVAL
        maps.append({
            "x_all": x_all, "x_own": x_own,
            "wq_mv": wq_mv, "wk_st": wk_st, "wv_sb": wv_sb, "wo_st": wo_st,
            "dw_sbh": dw_sbh, "up_sbh": up_sbh, "rt_sbh": rt_sbh,
            "gnw2": gnw2, "gnb2": gnb2,
            "lq1": lq1, "lk1": lk1, "lq2": lq2, "lk2": lk2,
            "ident": ident, "ones_col": ones_col, "ones_row": ones_row,
            "rep8": rep8, "maskt": maskt,
            "cosq": cq, "cosk": cosk,
        })
    return maps


def gather_output(cfg, results):
    B, T, D, NO, NT = cfg["B"], cfg["T"], cfg["D"], cfg["NO"], cfg["NT"]
    out = np.empty((B, T, D), dtype=np.float32)
    for core, r in enumerate(results):
        b, par = core // 2, core % 2
        oo = np.asarray(r["out_own"]).reshape(NO, 128, D)
        for p in range(NO):
            t0 = (2 * p + par) * 128
            out[b, t0:t0 + 128, :] = oo[p]
    return out



class Ctx:

    def __init__(self, engname, handle, counts, sems):
        self.eng = engname
        self.h = handle
        self.counts = counts
        self.sems = sems

    def is_(self, name):
        return self.eng == name

    def tick(self, sem, n=1, inst=None):
        self.counts[sem] += n
        if inst is not None:
            inst.then_inc(self.sems[sem], n)
        return self.counts[sem]

    def wait(self, eng, sem, val):
        if self.eng == eng:
            self.h.wait_ge(self.sems[sem], val)


def _finish_early(nc, sems, counts):
    with nc.Block("fin") as blk:
        @blk.sync
        def _(sync):
            for s in ("soy", "sox10", "sox11", "soq0", "soq1", "sov0", "sov1", "soa0", "soa1", "soo", "socc"):
                if counts.get(s):
                    sync.wait_ge(sems[s], counts[s])


def build_program(cfg, debug=False, stop_after=None):
    B, T, D, H, KV = cfg["B"], cfg["T"], cfg["D"], cfg["H"], cfg["KV"]
    NT, NO, DC, HQ, KC = cfg["NT"], cfg["NO"], cfg["DC"], cfg["HQ"], cfg["KC"]
    TOWN, QC, KFC = cfg["TOWN"], cfg["QC"], cfg["KFC"]
    SC = cfg["scale"]
    H2 = 2 * H

    nc = bass.Bass("TRN2", target_bir_lowering=False)

    def din(name, shape):
        return nc.dram_tensor(name, list(shape), F32, kind="ExternalInput")

    x_all = din("x_all", (T, D))
    x_own = din("x_own", (TOWN, D))
    wq_mv = din("wq_mv", (128, DC * HQ))
    wk_st = din("wk_st", (128, DC * KC))
    wv_sb_d = din("wv_sb", (128, DC * KC))
    wo_st = din("wo_st", (128, QC * D))
    dw_d = din("dw_sbh", (128, DC * 128))
    up_d = din("up_sbh", (128, DC * 128))
    rt_d = din("rt_sbh", (128, DC * 8))
    gnw_d = din("gnw2", (128, H2))
    gnb_d = din("gnb2", (128, H2))
    lq1_d = din("lq1", (1, 128)); lk1_d = din("lk1", (1, 128))
    lq2_d = din("lq2", (1, 128)); lk2_d = din("lk2", (1, 128))
    ident_d = din("ident", (128, 128))
    onesc_d = din("ones_col", (128, 1))
    onesr_d = din("ones_row", (1, 128))
    rep8_d = din("rep8", (8, 128))
    maskt_d = din("maskt", (128, 256))
    cosq_d = din("cosq", (128, TOWN))
    cosk_d = din("cosk", (128, T))

    out_own = nc.dram_tensor("out_own", [TOWN, D], F32, kind="ExternalOutput")

    NG4 = NT // 4
    y2t_scr = nc.dram_tensor("y2t_scr", [NG4, 128, DC * 512], F32, kind="Internal")
    v_scr = nc.dram_tensor("v_scr", [NT, 128, KC], F32, kind="Internal")
    q_scr = nc.dram_tensor("q_scr", [QC, 128, TOWN], F32, kind="Internal")
    x1own_scr = nc.dram_tensor("x1own_scr", [NO, 128, D], F32, kind="Internal")
    att_scr = nc.dram_tensor("att_scr", [H, NO, 2, 128, 128], F32, kind="Internal")
    cc_in = nc.dram_tensor("cc_in", [1, H2], F32, kind="Internal")
    cc_out = nc.dram_tensor("cc_out", [1, H2], F32, kind="Internal")

    top = contextlib.ExitStack()
    sb = lambda name, shape, dt=F32, st=top: st.enter_context(nc.sbuf_tensor(name, list(shape), dt))
    ps = lambda name, shape, dt=F32, st=top: st.enter_context(nc.psum_tensor(name, list(shape), dt))

    sems = {}
    for s in ("din", "dx0", "dx1", "dwt0", "dwt1", "dy", "dq0", "dq1", "dv3",
              "dyh0", "dyh1", "dx1t", "dst4", "dcc",
              "soy", "sox10", "sox11", "soq0", "soq1", "sov0", "sov1", "soa0", "soa1", "soo", "socc",
              "pe", "ve", "sc", "cc", "gp"):
        sems[s] = top.enter_context(nc.semaphore(s))
    counts = {k: 0 for k in sems}

    STORE_SEMS = ("din", "dx0", "dx1", "dwt0", "dwt1", "dy", "dq0", "dq1", "dv3",
                  "dyh0", "dyh1", "dx1t", "dst4", "dcc",
                  "soy", "sox10", "sox11", "soq0", "soq1", "sov0", "sov1", "soa0", "soa1", "soo", "socc")

    def phase(name, body):
        finals = []
        entry = {s: counts[s] for s in STORE_SEMS}
        with nc.Block(name) as blk:
            for engname in ENGS:
                def mk(en):
                    def section(h):
                        if en in ("sync", "vector", "scalar"):
                            for s_, v_ in entry.items():
                                if v_:
                                    h.wait_ge(sems[s_], v_)
                        ctx = Ctx(en, h, dict(counts), sems)
                        body(ctx)
                        finals.append(ctx.counts)
                    return section
                getattr(blk, engname)(mk(engname))
        ref = finals[0]
        for f in finals[1:]:
            assert f == ref, f"phase {name}: engine counter divergence {ref} vs {f}"
        counts.update(ref)

    ident = sb("sb_ident", (128, 128))
    onesc = sb("sb_onesc", (128, 1))
    onesr = sb("sb_onesr", (1, 128))
    rep8 = sb("sb_rep8", (8, 128))
    maskt = sb("sb_maskt", (128, 256))
    dw_sb = sb("sb_dw_sb", (128, DC * 128))
    up_sb = sb("sb_up_sb", (128, DC * 128))
    rt_sb = sb("sb_rt_sb", (128, DC * 8))
    gnw_sb = sb("sb_gnw_sb", (128, H2)); gnb_sb = sb("sb_gnb_sb", (128, H2))
    lam_misc = sb("lam_misc", (1, 128))
    lamneg_bc = sb("lamneg_bc", (128, 1))
    statp_s = sb("statp_s", (128, H)); statp_q = sb("statp_q", (128, H))
    scale_vec = sb("scale_vec", (128, H2)); bias_vec = sb("bias_vec", (128, H2))

    st0 = contextlib.ExitStack()
    lrow = sb("lrow", (1, 128), st=st0)
    l2row = sb("l2row", (1, 128), st=st0)
    lvals = sb("lvals", (1, 8), st=st0)
    pb_p0 = ps("p0_bc", (128, 1), st=st0)

    def p0(ctx):
        if ctx.is_("scalar"):
            from concourse.hw_specs import get_activation_tables
            idx = list(get_activation_tables(nc.m.arch)).index("natural_log_exp_and_others")
            ctx.h.add_instruction(mybir.InstLoadActFuncSet(
                name=nc.get_next_instruction_name(), act_func_set_id=idx, ins=[], outs=[]))
        loads = [
            (ident, ident_d), (onesc, onesc_d), (onesr, onesr_d), (rep8, rep8_d),
            (maskt, maskt_d),
            (dw_sb, dw_d), (up_sb, up_d), (rt_sb, rt_d),
            (gnw_sb, gnw_d), (gnb_sb, gnb_d),
        ]
        for dst, src in loads:
            i = ctx.h.dma_start(dst[:], src.ap()) if ctx.is_("sync") else None
            ctx.tick("din", 16, i)
        for dst, src in ((lrow, lq1_d), (l2row, lk1_d)):
            i = ctx.h.dma_start(dst[:], src.ap()) if ctx.is_("sync") else None
            ctx.tick("din", 16, i)
        w = counts_val = ctx.counts["din"]
        ctx.wait("vector", "din", w)
        i = ctx.h.tensor_mul(lam_misc[:], lrow[:], l2row[:]) if ctx.is_("vector") else None
        v1 = ctx.tick("ve", 1, i)
        ctx.wait("vector", "ve", v1)
        i = ctx.h.reduce_sum(lvals[:, 0:1], lam_misc[:], axis=AX.X) if ctx.is_("vector") else None
        v2 = ctx.tick("ve", 1, i)
        ctx.wait("sync", "ve", v2)
        for dst, src in ((lrow, lq2_d), (l2row, lk2_d)):
            i = ctx.h.dma_start(dst[:], src.ap()) if ctx.is_("sync") else None
            ctx.tick("din", 16, i)
        w = ctx.counts["din"]
        ctx.wait("vector", "din", w)
        ctx.wait("vector", "ve", v2)
        i = ctx.h.tensor_mul(lam_misc[:], lrow[:], l2row[:]) if ctx.is_("vector") else None
        v3 = ctx.tick("ve", 1, i)
        ctx.wait("vector", "ve", v3)
        i = ctx.h.reduce_sum(lvals[:, 1:2], lam_misc[:], axis=AX.X) if ctx.is_("vector") else None
        v4 = ctx.tick("ve", 1, i)
        ctx.wait("scalar", "ve", v4)
        i = ctx.h.activation(lvals[:, 2:4], lvals[:, 0:2], AF.Exp) if ctx.is_("scalar") else None
        s1 = ctx.tick("sc", 1, i)
        ctx.wait("vector", "sc", s1)
        i = ctx.h.tensor_sub(lvals[:, 4:5], lvals[:, 3:4], lvals[:, 2:3]) if ctx.is_("vector") else None
        v5 = ctx.tick("ve", 1, i)
        ctx.wait("vector", "ve", v5)
        i = ctx.h.tensor_single_scalar(lvals[:, 5:6], lvals[:, 4:5], float(LAM_INIT), ALU.subtract) if ctx.is_("vector") else None
        v6 = ctx.tick("ve", 1, i)
        ctx.wait("tensor", "ve", v6)
        ctx.wait("tensor", "din", ctx.counts["din"])
        pb = pb_p0
        i = ctx.h.matmul(pb[:], onesr[:], lvals[:, 5:6], start=True, stop=True) if ctx.is_("tensor") else None
        t1 = ctx.tick("pe", 1, i)
        ctx.wait("vector", "pe", t1)
        i = ctx.h.tensor_copy(lamneg_bc[:], pb[:]) if ctx.is_("vector") else None
        ctx.tick("ve", 1, i)

    phase("p0", p0)
    if stop_after == "p0":
        _finish_early(nc, sems, counts)
        return nc
    st0.close()

    _ffn_seq = [0]

    def make_ffn_tiles(st, GS):
        t = {"GS": GS}
        n = _ffn_seq[0]; _ffn_seq[0] += 1
        pfx = f"ffn{n}_"
        W = GS * 128
        def sb2(name, shape):
            return sb(pfx + name, shape, st=st)
        def ps2(name, shape):
            return ps(pfx + name, shape, st=st)
        t["X4"] = sb2("X4", (128, GS * D))
        t["XR4"] = sb2("XR4", (128, GS * D))
        t["YT4"] = sb2("YT4", (128, DC * W))
        t["Y"] = [sb2("Y0", (128, D)), sb2("Y1", (128, D))]
        t["f_sb"] = [sb2("f0", (128, W)), sb2("f1", (128, W))]
        t["st4"] = sb2("st4", (128, 24))
        t["e8"] = sb2("e8", (128, 6 * 8 * GS + 64))
        t["g4"] = sb2("g4", (128, 8 * GS))
        t["lg4"] = sb2("lg4", (8, W))
        t["g84"] = sb2("g84", (8, W))
        t["a4"] = sb2("a4", (128, W))
        t["ag4"] = sb2("ag4", (128, W))
        t["ps_a"] = ps2("ps_a", (128, 512))
        t["ps_lg"] = ps2("ps_lg", (8, 512))
        t["ps_h"] = ps2("ps_h", (128, 512))
        t["ps_g"] = ps2("ps_g", (128, 512))
        t["ps_up"] = ps2("ps_up", (128, 1024))
        t["ps_t"] = ps2("ps_t", (128, 512))
        t["state"] = {}
        return t

    def ffn_reset(t):
        t["state"] = {"pst_ve": 0, "psa_ve": 0, "pst_sc": 0, "psa_sc": 0,
                      "yt_sc": 0, "psh_sc": 0,
                      "psg_ve": 0, "psup_ve": {}, "y_free": {}}

    def emit_ln_group(ctx, t, SRC4, DUMP4, YDST, ngroups, src_ready=None):
        GS = ngroups
        st = t["st4"]
        s4 = SRC4.rearrange("p (a w) -> p a w", a=GS)
        d4 = DUMP4.rearrange("p (a w) -> p a w", a=GS)
        if src_ready is not None:
            ctx.wait("vector", src_ready[0], src_ready[1])
            ctx.wait("scalar", src_ready[0], src_ready[1])
        i = ctx.h.reduce_sum(st[:, 0:GS], s4, axis=AX.X, negate=True) if ctx.is_("vector") else None
        v = ctx.tick("ve", 1, i)
        i = ctx.h.activation(DUMP4, SRC4, AF.Square) if ctx.is_("scalar") else None
        s = ctx.tick("sc", 1, i)
        ctx.wait("vector", "sc", s)
        i = ctx.h.reduce_sum(st[:, 4:4 + GS], d4, axis=AX.X) if ctx.is_("vector") else None
        v = ctx.tick("ve", 1, i)
        ctx.wait("vector", "ve", v)
        i = ctx.h.tensor_scalar(st[:, 0:GS], st[:, 0:GS], 1.0 / D, None, ALU.mult) if ctx.is_("vector") else None
        v = ctx.tick("ve", 1, i)
        ctx.wait("vector", "ve", v)
        i = ctx.h.tensor_scalar(st[:, 4:4 + GS], st[:, 4:4 + GS], 1.0 / D, None, ALU.mult) if ctx.is_("vector") else None
        v = ctx.tick("ve", 1, i)
        ctx.wait("vector", "ve", v)
        i = ctx.h.tensor_mul(st[:, 8:8 + GS], st[:, 0:GS], st[:, 0:GS]) if ctx.is_("vector") else None
        v = ctx.tick("ve", 1, i)
        ctx.wait("vector", "ve", v)
        i = ctx.h.tensor_sub(st[:, 8:8 + GS], st[:, 4:4 + GS], st[:, 8:8 + GS]) if ctx.is_("vector") else None
        v = ctx.tick("ve", 1, i)
        ctx.wait("vector", "ve", v)
        i = ctx.h.tensor_single_scalar(st[:, 8:8 + GS], st[:, 8:8 + GS], float(EPS), ALU.add) if ctx.is_("vector") else None
        v = ctx.tick("ve", 1, i)
        ctx.wait("scalar", "ve", v)
        i = ctx.h.activation(st[:, 12:12 + GS], st[:, 8:8 + GS], AF.Ln) if ctx.is_("scalar") else None
        s = ctx.tick("sc", 1, i)
        ctx.wait("scalar", "sc", s)
        i = ctx.h.activation(st[:, 12:12 + GS], st[:, 12:12 + GS], AF.Exp, scale=-0.5) if ctx.is_("scalar") else None
        s = ctx.tick("sc", 1, i)
        ctx.wait("vector", "sc", s)
        i = ctx.h.tensor_mul(st[:, 16:16 + GS], st[:, 0:GS], st[:, 12:12 + GS]) if ctx.is_("vector") else None
        v_stat = ctx.tick("ve", 1, i)
        for tt in range(GS):
            ctx.wait("vector", "ve", v_stat)
            yfree = t["state"]["y_free"].get(tt % 2)
            if yfree:
                ctx.wait("vector", "pe", yfree)
            yb = t["Y"][tt % 2]
            i = ctx.h.tensor_scalar(yb[:], SRC4[:, tt * D:(tt + 1) * D],
                                    st[:, 12 + tt:13 + tt], st[:, 16 + tt:17 + tt],
                                    ALU.mult, ALU.add) if ctx.is_("vector") else None
            v = ctx.tick("ve", 1, i)
            YDST(tt, yb, v)
        return v_stat

    def emit_y_transposes(ctx, t, tt, yb, y_ve, YT4AP, W, dst_fn=None):
        ctx.wait("tensor", "ve", y_ve)
        last_pe = None
        for q4_ in range((DC + 3) // 4):
            jmax = min(4, DC - q4_ * 4)
            buf = t["ps_a"] if q4_ % 2 == 0 else t["ps_t"]
            key = "psa_sc" if q4_ % 2 == 0 else "pst_sc"
            if ctx.is_("tensor") and t["state"].get(key):
                ctx.h.wait_ge(sems["sc"], t["state"][key])
            if ctx.is_("tensor") and t["state"][key.replace("_sc", "_ve")]:
                ctx.h.wait_ge(sems["ve"], t["state"][key.replace("_sc", "_ve")])
            for j in range(jmax):
                dc = q4_ * 4 + j
                i = ctx.h.transpose(buf[:, j * 128:(j + 1) * 128],
                                    yb[:, dc * 128:(dc + 1) * 128], ident[:]) if ctx.is_("tensor") else None
                pe = ctx.tick("pe", 1, i)
            last_pe = pe
            ctx.wait("scalar", "pe", pe)
            if dst_fn is not None:
                dst = dst_fn(q4_)
            else:
                dst = YT4AP[:, q4_ * 4 * W:(q4_ * 4 + jmax) * W].rearrange(
                    "p (a w) -> p a w", w=W)[:, :, tt * 128:(tt + 1) * 128]
            srcb = buf[:, 0:jmax * 128].rearrange("p (a w) -> p a w", w=128)
            i = ctx.h.copy(dst, srcb) if ctx.is_("scalar") else None
            t["state"][key] = ctx.tick("sc", 1, i)
            t["state"]["yt_sc"] = t["state"][key]
        t["state"]["y_free"][tt % 2] = last_pe
        return t["state"]["psa_ve"]

    def emit_ffn_group(ctx, t, GS, res_scale=0.5):
        W = GS * 128
        stt_state = t["state"]
        g4 = t["g4"]; e8 = t["e8"]; lg4 = t["lg4"]; g84 = t["g84"]
        ctx.wait("tensor", "sc", stt_state["yt_sc"])
        for dc in range(DC):
            i = ctx.h.matmul(t["ps_lg"][:, 0:W], rt_sb[:, dc * 8:(dc + 1) * 8],
                             t["YT4"][:, dc * W:(dc + 1) * W],
                             start=(dc == 0), stop=(dc == DC - 1)) if ctx.is_("tensor") else None
            pe = ctx.tick("pe", 1, i)
        ctx.wait("vector", "pe", pe)
        i = ctx.h.tensor_copy(lg4[:, 0:W], t["ps_lg"][:, 0:W]) if ctx.is_("vector") else None
        v = ctx.tick("ve", 1, i)
        if ctx.is_("tensor") and stt_state["psh_sc"]:
            ctx.h.wait_ge(sems["sc"], stt_state["psh_sc"])
        for dc in range(DC):
            i = ctx.h.matmul(t["ps_h"][:, 0:W], dw_sb[:, dc * 128:(dc + 1) * 128],
                             t["YT4"][:, dc * W:(dc + 1) * W],
                             start=(dc == 0), stop=(dc == DC - 1)) if ctx.is_("tensor") else None
            pe_h = ctx.tick("pe", 1, i)
        ctx.wait("tensor", "ve", v)
        ctx.wait("tensor", "ve", stt_state["pst_ve"])
        if stt_state.get("pst_sc"):
            ctx.wait("tensor", "sc", stt_state["pst_sc"])
        for tt in range(GS):
            i = ctx.h.transpose(t["ps_t"][:, tt * 8:(tt + 1) * 8],
                                lg4[:, tt * 128:(tt + 1) * 128], ident[0:8, 0:8]) if ctx.is_("tensor") else None
            pe = ctx.tick("pe", 1, i)
        ctx.wait("vector", "pe", pe)
        i = ctx.h.tensor_copy(g4[:, 0:8 * GS].rearrange("p (a w) -> p a w", w=8),
                              t["ps_t"][:, 0:8 * GS].rearrange("p (a w) -> p a w", w=8)) if ctx.is_("vector") else None
        v = ctx.tick("ve", 1, i)
        stt_state["pst_ve"] = v
        G8 = 8 * GS
        ctx.wait("scalar", "ve", v)
        i = ctx.h.activation(e8[:, 0:G8], g4[:, 0:G8], AF.Exp) if ctx.is_("scalar") else None
        s = ctx.tick("sc", 1, i)
        ctx.wait("vector", "sc", s)
        i = ctx.h.reduce_sum(e8[:, 4 * G8: 4 * G8 + GS], e8[:, 0:G8].rearrange("p (a w) -> p a w", w=8),
                             axis=AX.X) if ctx.is_("vector") else None
        v = ctx.tick("ve", 1, i)
        ctx.wait("vector", "ve", v)
        i = ctx.h.reciprocal(e8[:, 4 * G8 + 4: 4 * G8 + 4 + GS], e8[:, 4 * G8:4 * G8 + GS]) if ctx.is_("vector") else None
        v = ctx.tick("ve", 1, i)
        for tt in range(GS):
            ctx.wait("vector", "ve", v)
            i = ctx.h.tensor_scalar(e8[:, G8 + tt * 8: G8 + (tt + 1) * 8], e8[:, tt * 8:(tt + 1) * 8],
                                    e8[:, 4 * G8 + 4 + tt:4 * G8 + 5 + tt], None, ALU.mult) if ctx.is_("vector") else None
            vw = ctx.tick("ve", 1, i)
        ctx.wait("vector", "ve", vw)
        i = ctx.h.reduce_max(e8[:, 4 * G8 + 8:4 * G8 + 8 + GS],
                             e8[:, G8:2 * G8].rearrange("p (a w) -> p a w", w=8), axis=AX.X) if ctx.is_("vector") else None
        v = ctx.tick("ve", 1, i)
        for tt in range(GS):
            ctx.wait("vector", "ve", v)
            i = ctx.h.tensor_single_scalar(e8[:, 2 * G8 + tt * 8:2 * G8 + (tt + 1) * 8],
                                           e8[:, G8 + tt * 8:G8 + (tt + 1) * 8],
                                           e8[:, 4 * G8 + 8 + tt:4 * G8 + 9 + tt], ALU.is_ge) if ctx.is_("vector") else None
            vm = ctx.tick("ve", 1, i)
        ctx.wait("vector", "ve", vm)
        i = ctx.h.tensor_sub(e8[:, 3 * G8:4 * G8], e8[:, G8:2 * G8], e8[:, 2 * G8:3 * G8]) if ctx.is_("vector") else None
        v = ctx.tick("ve", 1, i)
        ctx.wait("vector", "ve", v)
        i = ctx.h.reduce_max(e8[:, 4 * G8 + 12:4 * G8 + 12 + GS],
                             e8[:, 3 * G8:4 * G8].rearrange("p (a w) -> p a w", w=8), axis=AX.X) if ctx.is_("vector") else None
        v = ctx.tick("ve", 1, i)
        for tt in range(GS):
            ctx.wait("vector", "ve", v)
            i = ctx.h.tensor_single_scalar(e8[:, 2 * G8 + tt * 8:2 * G8 + (tt + 1) * 8],
                                           e8[:, G8 + tt * 8:G8 + (tt + 1) * 8],
                                           e8[:, 4 * G8 + 12 + tt:4 * G8 + 13 + tt], ALU.is_ge) if ctx.is_("vector") else None
            vm = ctx.tick("ve", 1, i)
        ctx.wait("vector", "ve", vm)
        i = ctx.h.tensor_mul(g4[:, 0:G8], e8[:, G8:2 * G8], e8[:, 2 * G8:3 * G8]) if ctx.is_("vector") else None
        v_gate = ctx.tick("ve", 1, i)
        ctx.wait("tensor", "ve", v_gate)
        for tt in range(GS):
            i = ctx.h.transpose(t["ps_lg"][0:8, tt * 128:(tt + 1) * 128],
                                g4[:, tt * 8:(tt + 1) * 8], ident[:]) if ctx.is_("tensor") else None
            pe = ctx.tick("pe", 1, i)
        ctx.wait("vector", "pe", pe)
        i = ctx.h.tensor_copy(g84[:, 0:W], t["ps_lg"][0:8, 0:W]) if ctx.is_("vector") else None
        v = ctx.tick("ve", 1, i)
        ctx.wait("tensor", "ve", v)
        if ctx.is_("tensor") and stt_state["psg_ve"]:
            ctx.h.wait_ge(sems["ve"], stt_state["psg_ve"])
        i = ctx.h.matmul(t["ps_g"][:, 0:W], rep8[:], g84[:, 0:W], start=True, stop=True) if ctx.is_("tensor") else None
        pe_g = ctx.tick("pe", 1, i)
        ctx.wait("scalar", "pe", pe_h)
        i = ctx.h.activation(t["a4"][:, 0:W], t["ps_h"][:, 0:W], AF.Exp, scale=-1.0) if ctx.is_("scalar") else None
        s = ctx.tick("sc", 1, i)
        stt_state["psh_sc"] = s
        ctx.wait("vector", "sc", s)
        i = ctx.h.tensor_single_scalar(t["a4"][:, 0:W], t["a4"][:, 0:W], 1.0, ALU.add) if ctx.is_("vector") else None
        v = ctx.tick("ve", 1, i)
        ctx.wait("vector", "ve", v)
        i = ctx.h.reciprocal(t["a4"][:, 0:W], t["a4"][:, 0:W]) if ctx.is_("vector") else None
        v = ctx.tick("ve", 1, i)
        ctx.wait("vector", "ve", v)
        i = ctx.h.tensor_mul(t["a4"][:, 0:W], t["a4"][:, 0:W], t["ps_h"][:, 0:W]) if ctx.is_("vector") else None
        v = ctx.tick("ve", 1, i)
        ctx.wait("vector", "ve", v)
        ctx.wait("vector", "pe", pe_g)
        i = ctx.h.tensor_mul(t["ag4"][:, 0:W], t["a4"][:, 0:W], t["ps_g"][:, 0:W]) if ctx.is_("vector") else None
        v_ag = ctx.tick("ve", 1, i)
        stt_state["psg_ve"] = v_ag
        X4v = t["X4"][:].rearrange("p (a w) -> p a w", a=GS)
        XR4v = t["XR4"][:].rearrange("p (a w) -> p a w", a=GS)
        ctx.wait("tensor", "ve", v_ag)
        evac = t["state"].setdefault("psup_sc", {})
        evac_by_m = {}
        v_last = v_ag
        for m in range(DC + 1):
            if m < DC:
                if ctx.is_("tensor") and (m % 2) in evac:
                    ctx.h.wait_ge(sems["sc"], evac[m % 2])
                if ctx.is_("tensor") and (m % 2) in t["state"]["psup_ve"]:
                    ctx.h.wait_ge(sems["ve"], t["state"]["psup_ve"][m % 2])
                i = ctx.h.matmul(t["ps_up"][:, (m % 2) * 512:(m % 2) * 512 + W],
                                 up_sb[:, m * 128:(m + 1) * 128], t["ag4"][:, 0:W],
                                 start=True, stop=True) if ctx.is_("tensor") else None
                pe_up = ctx.tick("pe", 1, i)
                ctx.wait("scalar", "pe", pe_up)
                i = ctx.h.copy(t["f_sb"][m % 2][:, 0:W],
                               t["ps_up"][:, (m % 2) * 512:(m % 2) * 512 + W]) if ctx.is_("scalar") else None
                evac[m % 2] = ctx.tick("sc", 1, i)
                evac_by_m[m] = evac[m % 2]
            if m >= 1:
                mm = m - 1
                tb = t["ps_t"] if mm % 2 == 0 else t["ps_a"]
                tkey = "pst_ve" if mm % 2 == 0 else "psa_ve"
                tskey = "pst_sc" if mm % 2 == 0 else "psa_sc"
                ctx.wait("tensor", "sc", evac_by_m[mm])
                ctx.wait("tensor", "ve", stt_state[tkey])
                if stt_state.get(tskey):
                    ctx.wait("tensor", "sc", stt_state[tskey])
                for j in range(GS):
                    i = ctx.h.transpose(tb[:, j * 128:(j + 1) * 128],
                                        t["f_sb"][mm % 2][:, j * 128:(j + 1) * 128], ident[:]) if ctx.is_("tensor") else None
                    pe_t = ctx.tick("pe", 1, i)
                ctx.wait("vector", "pe", pe_t)
                i = ctx.h.scalar_tensor_tensor(XR4v[:, :, mm * 128:(mm + 1) * 128],
                                               tb[:, 0:W].rearrange("p (a w) -> p a w", w=128),
                                               float(res_scale),
                                               X4v[:, :, mm * 128:(mm + 1) * 128],
                                               ALU.mult, ALU.add) if ctx.is_("vector") else None
                v_last = ctx.tick("ve", 1, i)
                stt_state[tkey] = v_last
        return v_last

    st1 = contextlib.ExitStack()
    t1 = make_ffn_tiles(st1, 4)

    def p1a(ctx):
        ffn_reset(t1)
        prev_ve = prev_sc = 0
        ystore = {}
        for grp in range(NG4):
            if grp > 0:
                ctx.wait("sync", "ve", prev_ve)
                ctx.wait("sync", "sc", prev_sc)
                ctx.wait("scalar", "ve", prev_ve)
                ctx.wait("vector", "soy", ystore[grp - 1])
            i = ctx.h.dma_start(
                t1["X4"][:].rearrange("p (a w) -> p a w", a=4),
                x_all.ap()[grp * 512:(grp + 1) * 512, :].rearrange("(a q) w -> q a w", a=4),
            ) if ctx.is_("sync") else None
            dv = ctx.tick("dx0", 16, i)

            def ydst1(tt, yb, y_ve):
                emit_y_transposes(ctx, t1, tt, yb, y_ve, t1["YT4"][:], 512)
            emit_ln_group(ctx, t1, t1["X4"][:], t1["XR4"][:], ydst1, 4, src_ready=("dx0", dv))
            v_xr = emit_ffn_group(ctx, t1, 4)
            ctx.wait("vector", "pe", ctx.counts["pe"])
            def ydst2(tt, yb, y_ve):
                emit_y_transposes(ctx, t1, tt, yb, y_ve, t1["YT4"][:], 512)
            emit_ln_group(ctx, t1, t1["XR4"][:], t1["X4"][:], ydst2, 4, src_ready=("ve", v_xr))
            ctx.wait("sync", "sc", t1["state"]["yt_sc"])
            i = ctx.h.dma_start(y2t_scr.ap()[grp, :, :], t1["YT4"][:]) if ctx.is_("sync") else None
            ystore[grp] = ctx.tick("soy", 16, i)
            prev_ve = ctx.counts["ve"]
            prev_sc = ctx.counts["sc"]

    phase("p1a", p1a)
    st1.close()

    st2 = contextlib.ExitStack()
    y2ownT = sb("y2ownT", (128, DC * TOWN), st=st2)

    st2a = contextlib.ExitStack()
    t2 = make_ffn_tiles(st2a, 2)
    NOG = NO // 2

    def p2a(ctx):
        ffn_reset(t2)
        prev_ve = prev_sc = 0
        x1store = {}
        for grp in range(NOG):
            if grp > 0:
                ctx.wait("sync", "ve", prev_ve)
                ctx.wait("sync", "sc", prev_sc)
                ctx.wait("scalar", "ve", prev_ve)
                sx = f"sox1{(grp - 1) % 2}"
                ctx.wait("scalar", sx, x1store[grp - 1])
                ctx.wait("vector", sx, x1store[grp - 1])
            i = ctx.h.dma_start(
                t2["X4"][:].rearrange("p (a w) -> p a w", a=2),
                x_own.ap()[grp * 256:(grp + 1) * 256, :].rearrange("(a q) w -> q a w", a=2),
            ) if ctx.is_("sync") else None
            dv = ctx.tick("dx0", 16, i)

            def ydst1(tt, yb, y_ve):
                emit_y_transposes(ctx, t2, tt, yb, y_ve, t2["YT4"][:], 256)
            emit_ln_group(ctx, t2, t2["X4"][:], t2["XR4"][:], ydst1, 2, src_ready=("dx0", dv))
            v_xr = emit_ffn_group(ctx, t2, 2)
            ctx.wait("sync", "ve", v_xr)
            i = ctx.h.dma_start(
                x1own_scr.ap()[grp * 2:(grp + 1) * 2, :, :].rearrange("a p w -> p a w"),
                t2["XR4"][:].rearrange("p (a w) -> p a w", a=2),
            ) if ctx.is_("sync") else None
            x1store[grp] = ctx.tick(f"sox1{grp % 2}", 16, i)
            ctx.wait("vector", "pe", ctx.counts["pe"])
            def ydst2(tt, yb, y_ve, _g=grp):
                c = _g * 2 + tt
                def dst_into_y2own(q4_):
                    jm = min(4, DC - q4_ * 4)
                    return y2ownT[:, q4_ * 4 * TOWN:(q4_ * 4 + jm) * TOWN].rearrange(
                        "p (a w) -> p a w", w=TOWN)[:, :, c * 128:(c + 1) * 128]
                emit_y_transposes(ctx, t2, tt, yb, y_ve, None, 256, dst_fn=dst_into_y2own)
            emit_ln_group(ctx, t2, t2["XR4"][:], t2["X4"][:], ydst2, 2, src_ready=("ve", v_xr))
            prev_ve = ctx.counts["ve"]
            prev_sc = ctx.counts["sc"]

    phase("p2a", p2a)
    st2a.close()

    st2b = contextlib.ExitStack()
    cossin_q = sb("sb_cosq", (128, TOWN), st=st2b)
    cosq = cossin_q[0:64, :]; sinq = cossin_q[64:128, :]
    NQG = QC // 4
    wqg = [sb("wqg0", (128, DC * 512), st=st2b), sb("wqg1", (128, DC * 512), st=st2b)]
    qst = [sb("qst0", (128, TOWN), st=st2b), sb("qst1", (128, TOWN), st=st2b)]
    rqtmp = sb("rqtmp", (64, TOWN), st=st2b)
    rqtmp2 = sb("rqtmp2", (64, TOWN), st=st2b)
    ps_q = [ps("ps_q0", (128, TOWN), st=st2b), ps("ps_q1", (128, TOWN), st=st2b)]
    NTP = (TOWN + 511) // 512

    def p2b(ctx):
        rope_done = {}
        store_done = {}
        wq_done = {}
        i = ctx.h.dma_start(cossin_q[:], cosq_d.ap()) if ctx.is_("sync") else None
        dcq = ctx.tick("din", 16, i)
        ctx.wait("vector", "din", dcq)
        for qg in range(NQG):
            wbuf = wqg[qg % 2]
            if ctx.is_("sync") and qg >= 2:
                ctx.h.wait_ge(sems["pe"], wq_done[qg - 2])
            wsem = f"dwt{qg % 2}"
            i = ctx.h.dma_start(
                wbuf[:].rearrange("p (dc w) -> p dc w", dc=DC),
                wq_mv.ap().rearrange("p (dc q) -> p dc q", dc=DC)[:, :, qg * 512:(qg + 1) * 512],
            ) if ctx.is_("sync") else None
            dv = ctx.tick(wsem, 16, i)
            ctx.wait("tensor", wsem, dv)
            for qc4 in range(4):
                qc = qg * 4 + qc4
                psq = ps_q[qc % 2]
                if ctx.is_("tensor") and qc >= 2:
                    ctx.h.wait_ge(sems["ve"], rope_done[qc - 2])
                for dc in range(DC):
                    for tp in range(NTP):
                        t0_ = tp * 512
                        tl = min(512, TOWN - t0_)
                        i = ctx.h.matmul(psq[:, t0_:t0_ + tl],
                                         wbuf[:, dc * 512 + qc4 * 128: dc * 512 + (qc4 + 1) * 128],
                                         y2ownT[:, dc * TOWN + t0_: dc * TOWN + t0_ + tl],
                                         start=(dc == 0), stop=(dc == DC - 1)) if ctx.is_("tensor") else None
                        pe = ctx.tick("pe", 1, i)
                wq_done[qg] = pe
                ctx.wait("vector", "pe", pe)
                qb = qst[qc % 2]
                if qc >= 2:
                    ctx.wait("vector", f"soq{qc % 2}", store_done[qc - 2])
                if qc > 0:
                    ctx.wait("vector", "ve", rope_done[qc - 1])
                ev = psq[0:64, :]; od = psq[64:128, :]
                i = ctx.h.tensor_mul(rqtmp[:], ev, cosq) if ctx.is_("vector") else None
                ctx.tick("ve", 1, i)
                i = ctx.h.tensor_mul(qb[0:64, :], od, sinq) if ctx.is_("vector") else None
                v = ctx.tick("ve", 1, i)
                ctx.wait("vector", "ve", v)
                i = ctx.h.tensor_sub(qb[0:64, :], rqtmp[:], qb[0:64, :]) if ctx.is_("vector") else None
                v = ctx.tick("ve", 1, i)
                ctx.wait("vector", "ve", v)
                i = ctx.h.tensor_mul(rqtmp[:], ev, sinq) if ctx.is_("vector") else None
                ctx.tick("ve", 1, i)
                i = ctx.h.tensor_mul(rqtmp2[:], od, cosq) if ctx.is_("vector") else None
                v = ctx.tick("ve", 1, i)
                ctx.wait("vector", "ve", v)
                i = ctx.h.tensor_add(qb[64:128, :], rqtmp[:], rqtmp2[:]) if ctx.is_("vector") else None
                v = ctx.tick("ve", 1, i)
                rope_done[qc] = v
                ctx.wait("sync", "ve", v)
                i = ctx.h.dma_start(q_scr.ap()[qc, :, :], qb[:]) if ctx.is_("sync") else None
                store_done[qc] = ctx.tick(f"soq{qc % 2}", 16, i)

    phase("p2b", p2b)
    if stop_after == "p2b":
        _finish_early(nc, sems, counts)
        return nc
    st2b.close()
    st2.close()

    stk = contextlib.ExitStack()
    k_sb = sb("k_sb", (128, KV * 2 * T), st=stk)

    st1b = contextlib.ExitStack()
    cossin_k = sb("sb_cosk", (128, T), st=st1b)
    wk_sbt = sb("wk_sbt", (128, DC * KC), st=st1b)
    y2g = sb("y2g", (128, DC * 512), st=st1b)
    rktmp = sb("rktmp", (64, 1024), st=st1b)
    ps_k = ps("ps_k", (128, 512), st=st1b)
    GT = 512 // 128
    NG = T // 512

    def p1b(ctx):
        soy_fin = ctx.counts["soy"]
        ctx.wait("sync", "soy", soy_fin)
        i = ctx.h.dma_start(cossin_k[:], cosk_d.ap()) if ctx.is_("sync") else None
        dck = ctx.tick("din", 16, i)
        ctx.wait("vector", "din", dck)
        i = ctx.h.dma_start(wk_sbt[:], wk_st.ap()) if ctx.is_("sync") else None
        dw_ = ctx.tick("dwt0", 16, i)
        for grp in range(NG):
            i = ctx.h.dma_start(y2g[:], y2t_scr.ap()[grp, :, :]) if ctx.is_("sync") else None
            dv = ctx.tick("dy", 16, i)
            ctx.wait("tensor", "dy", dv)
            if grp == 0:
                ctx.wait("tensor", "dwt0", dw_)
            for fc in range(KFC):
                if ctx.is_("tensor") and (grp, fc) != (0, 0):
                    ctx.h.wait_ge(sems["ve"], v_prev)
                for dc in range(DC):
                    i = ctx.h.matmul(ps_k[:], wk_sbt[:, dc * KC + fc * 128: dc * KC + (fc + 1) * 128],
                                     y2g[:, dc * 512:(dc + 1) * 512],
                                     start=(dc == 0), stop=(dc == DC - 1)) if ctx.is_("tensor") else None
                    pe = ctx.tick("pe", 1, i)
                ctx.wait("vector", "pe", pe)
                base = fc * T + grp * 512
                ct = cossin_k[0:64, grp * 512:grp * 512 + 512]
                stt = cossin_k[64:128, grp * 512:grp * 512 + 512]
                ev = ps_k[0:64, :]; od = ps_k[64:128, :]
                i = ctx.h.tensor_mul(rktmp[:, 0:512], ev, ct) if ctx.is_("vector") else None
                ctx.tick("ve", 1, i)
                i = ctx.h.tensor_mul(rktmp[:, 512:1024], od, stt) if ctx.is_("vector") else None
                v = ctx.tick("ve", 1, i)
                ctx.wait("vector", "ve", v)
                i = ctx.h.tensor_sub(k_sb[0:64, base:base + 512], rktmp[:, 0:512], rktmp[:, 512:1024]) if ctx.is_("vector") else None
                v = ctx.tick("ve", 1, i)
                ctx.wait("vector", "ve", v)
                i = ctx.h.tensor_mul(rktmp[:, 0:512], ev, stt) if ctx.is_("vector") else None
                ctx.tick("ve", 1, i)
                i = ctx.h.tensor_mul(rktmp[:, 512:1024], od, ct) if ctx.is_("vector") else None
                v = ctx.tick("ve", 1, i)
                ctx.wait("vector", "ve", v)
                i = ctx.h.tensor_add(k_sb[64:128, base:base + 512], rktmp[:, 0:512], rktmp[:, 512:1024]) if ctx.is_("vector") else None
                v_prev = ctx.tick("ve", 1, i)
            if ctx.is_("sync"):
                ctx.h.wait_ge(sems["pe"], pe)

    phase("p1b", p1b)
    if stop_after == "p1b":
        _finish_early(nc, sems, counts)
        return nc
    st1b.close()

    st1c = contextlib.ExitStack()
    wv_s = sb("wv_s", (128, DC * KC), st=st1c)
    y2g2 = sb("y2g2", (128, DC * 512), st=st1c)
    vstage = [sb("vstage0", (128, KC), st=st1c), sb("vstage1", (128, KC), st=st1c)]
    ps_v = ps("ps_v", (128, 2048), st=st1c)
    assert KC <= 1024

    def p1c(ctx):
        i = ctx.h.dma_start(wv_s[:], wv_sb_d.ap()) if ctx.is_("sync") else None
        dw_ = ctx.tick("dwt0", 16, i)
        NVP = (KC + 511) // 512
        evac_sc = {}
        vst = {}
        pe = 0
        for grp in range(NG4):
            if grp > 0:
                ctx.wait("sync", "pe", pe)
            i = ctx.h.dma_start(y2g2[:], y2t_scr.ap()[grp, :, :]) if ctx.is_("sync") else None
            dv = ctx.tick("dy", 16, i)
            ctx.wait("tensor", "dy", dv)
            if grp == 0:
                ctx.wait("tensor", "dwt0", dw_)
            for tt in range(4):
                c = grp * 4 + tt
                slot = c % 2
                pbase = slot * 1024
                if ctx.is_("tensor") and (c - 2) in evac_sc:
                    ctx.h.wait_ge(sems["sc"], evac_sc[c - 2])
                for dc in range(DC):
                    for vp in range(NVP):
                        w0 = vp * 512
                        wlen = min(512, KC - w0)
                        i = ctx.h.matmul(ps_v[:, pbase + w0: pbase + w0 + wlen],
                                         y2g2[:, dc * 512 + tt * 128: dc * 512 + (tt + 1) * 128],
                                         wv_s[:, dc * KC + w0: dc * KC + w0 + wlen],
                                         start=(dc == 0), stop=(dc == DC - 1)) if ctx.is_("tensor") else None
                        pe = ctx.tick("pe", 1, i)
                ctx.wait("scalar", "pe", pe)
                if (c - 2) in vst:
                    ctx.wait("scalar", f"sov{slot}", vst[c - 2])
                i = ctx.h.copy(vstage[slot][:], ps_v[:, pbase:pbase + KC]) if ctx.is_("scalar") else None
                evac_sc[c] = ctx.tick("sc", 1, i)
                ctx.wait("sync", "sc", evac_sc[c])
                i = ctx.h.dma_start(v_scr.ap()[c, :, :], vstage[slot][:]) if ctx.is_("sync") else None
                vst[c] = ctx.tick(f"sov{slot}", 16, i)

    phase("p1c", p1c)
    if stop_after == "p1c":
        _finish_early(nc, sems, counts)
        return nc
    st1c.close()

    st3 = contextlib.ExitStack()
    REP3 = H // KV
    v_g = [sb("v_g0", (128, NT * 256), st=st3), sb("v_g1", (128, NT * 256), st=st3)]
    q4b = [sb("q4b0", (128, REP3 * 256), st=st3), sb("q4b1", (128, REP3 * 256), st=st3)]
    e1_sb = [sb("e1a", (128, T), st=st3), sb("e1b", (128, T), st=st3), sb("e1c", (128, T), st=st3)]
    e2_sb = [sb("e2a", (128, T), st=st3), sb("e2b", (128, T), st=st3)]
    dT4 = sb("dT4", (128, NT * REP3 * 128), st=st3)
    nrow = sb("nrow3", (128, 32), st=st3)
    astage = [sb("astage0", (128, REP3 * 256), st=st3), sb("astage1", (128, REP3 * 256), st=st3)]
    sq3 = sb("sq3", (128, 256), st=st3)
    stat_tmp = sb("stat_tmp", (128, 4), st=st3)
    ps_seg = ps("ps_seg", (128, 4 * 512), st=st3)
    ps_t3 = ps("ps_t3", (128, 1024), st=st3)
    ps_av = ps("ps_av", (128, 1024), st=st3)

    def p3(ctx):
        i = ctx.h.memset(statp_s[:], 0.0) if ctx.is_("vector") else None
        ctx.tick("ve", 1, i)
        i = ctx.h.memset(statp_q[:], 0.0) if ctx.is_("vector") else None
        v = ctx.tick("ve", 1, i)
        ctx.wait("gpsimd", "ve", v)
        ctx.wait("sync", "sov0", ctx.counts["sov0"])
        ctx.wait("sync", "sov1", ctx.counts["sov1"])
        ctx.wait("sync", "soq0", ctx.counts["soq0"])
        ctx.wait("sync", "soq1", ctx.counts["soq1"])
        slot_sc = [0, 0, 0, 0]
        seg_i = [0]
        tpose_pe = {}
        dcopy_ve = {}
        d_ve = {}
        exp_sc = {}
        av_done_sc = 0
        astat_ve = {}
        prev_stat_ve = 0
        dvv_next = [0]
        gp_last = [0]
        qslot_pe = {}
        aout = {}
        it = 0

        def emit_scores(ctx, g, pos, h4, hctr, qsem_val, EC, itv):
            E = EC * 128
            par = hctr
            e1, e2 = e1_sb[par % 3], e2_sb[par % 2]
            for pol in range(2):
                esb = e1 if pol == 0 else e2
                kbase = (g * 2 + pol) * T
                nseg = (E + 511) // 512
                seg_order = [nseg - 1] + list(range(nseg - 1))
                for seg in seg_order:
                    s0 = seg * 512
                    slen = min(512, E - s0)
                    sl = seg_i[0] % 4
                    seg_i[0] += 1
                    pslot = ps_seg[:, sl * 512: sl * 512 + slen]
                    if ctx.is_("tensor"):
                        if slot_sc[sl]:
                            ctx.h.wait_ge(sems["sc"], slot_sc[sl])
                        ctx.h.wait_ge(sems[f"dq{itv % 2}"], qsem_val)
                    i = ctx.h.matmul(pslot, q4b[itv % 2][:, h4 * 256 + pol * 128: h4 * 256 + (pol + 1) * 128],
                                     k_sb[:, kbase + s0: kbase + s0 + slen],
                                     start=True, stop=True) if ctx.is_("tensor") else None
                    pe = ctx.tick("pe", 1, i)
                    is_last = (seg == nseg - 1)
                    if is_last:
                        m0 = slen - 256
                        ctx.wait("vector", "pe", pe)
                        i = ctx.h.tensor_add(pslot[:, m0:m0 + 256], pslot[:, m0:m0 + 256], maskt[:]) if ctx.is_("vector") else None
                        vm = ctx.tick("ve", 1, i)
                        ctx.wait("scalar", "ve", vm)
                    else:
                        ctx.wait("scalar", "pe", pe)
                    if seg == 0 and pol == 0:
                        prev = tpose_pe.get(par % 3)
                        if prev:
                            ctx.wait("scalar", "pe", prev)
                        if (hctr - 2) in d_ve:
                            ctx.wait("scalar", "ve", d_ve[hctr - 2])
                    acol = (par % 2) * 16 + pol * 4 + seg
                    i = ctx.h.activation(esb[:, s0:s0 + slen], pslot, AF.Exp,
                                         scale=float(SC), accum_out=nrow[:, acol:acol + 1]) if ctx.is_("scalar") else None
                    se = ctx.tick("sc", 1, i)
                    slot_sc[sl] = se
                exp_sc[(par % 2, pol)] = se

        def emit_combine(ctx, g, pos, h4, hctr, EC):
            E = EC * 128
            par = hctr
            e1, e2 = e1_sb[par % 3], e2_sb[par % 2]
            nb = (par % 2) * 16
            ctx.wait("vector", "sc", exp_sc[(par % 2, 1)])
            nseg = (E + 511) // 512
            i = ctx.h.reduce_sum(nrow[:, nb + 8:nb + 9], nrow[:, nb:nb + nseg], axis=AX.X) if ctx.is_("vector") else None
            v = ctx.tick("ve", 1, i)
            i = ctx.h.reduce_sum(nrow[:, nb + 9:nb + 10], nrow[:, nb + 4:nb + 4 + nseg], axis=AX.X) if ctx.is_("vector") else None
            v = ctx.tick("ve", 1, i)
            ctx.wait("vector", "ve", v)
            i = ctx.h.reciprocal(nrow[:, nb + 10:nb + 12], nrow[:, nb + 8:nb + 10]) if ctx.is_("vector") else None
            v = ctx.tick("ve", 1, i)
            ctx.wait("vector", "ve", v)
            i = ctx.h.tensor_mul(nrow[:, nb + 12:nb + 13], nrow[:, nb + 11:nb + 12], lamneg_bc[:]) if ctx.is_("vector") else None
            v = ctx.tick("ve", 1, i)
            ctx.wait("vector", "ve", v)
            i = ctx.h.tensor_scalar(e1[:, 0:E], e1[:, 0:E], nrow[:, nb + 10:nb + 11], None, ALU.mult) if ctx.is_("vector") else None
            v = ctx.tick("ve", 1, i)
            ctx.wait("vector", "ve", v)
            i = ctx.h.scalar_tensor_tensor(e1[:, 0:E], e2[:, 0:E], nrow[:, nb + 12:nb + 13],
                                           e1[:, 0:E], ALU.mult, ALU.add) if ctx.is_("vector") else None
            d_ve[hctr] = ctx.tick("ve", 1, i)

        def emit_transposes(ctx, g, pos, h4, hctr, EC):
            par = hctr
            e1 = e1_sb[par % 3]
            ctx.wait("tensor", "ve", d_ve[hctr])
            last = None
            for q4_ in range((EC + 3) // 4):
                jmax = min(4, EC - q4_ * 4)
                half = q4_ % 2
                if ctx.is_("tensor"):
                    prev = dcopy_ve.get(half)
                    if prev:
                        ctx.h.wait_ge(sems["sc"], prev)
                for j in range(jmax):
                    kc = q4_ * 4 + j
                    i = ctx.h.transpose(ps_t3[:, half * 512 + j * 128: half * 512 + (j + 1) * 128],
                                        e1[:, kc * 128:(kc + 1) * 128], ident[:]) if ctx.is_("tensor") else None
                    pe = ctx.tick("pe", 1, i)
                ctx.wait("scalar", "pe", pe)
                dst = dT4[:, q4_ * 4 * REP3 * 128: (q4_ * 4 + jmax) * REP3 * 128].rearrange(
                    "p (a w) -> p a w", w=REP3 * 128)[:, :, h4 * 128:(h4 + 1) * 128]
                srcb = ps_t3[:, half * 512: half * 512 + jmax * 128].rearrange("p (a w) -> p a w", w=128)
                i = ctx.h.copy(dst, srcb) if ctx.is_("scalar") else None
                last = ctx.tick("sc", 1, i)
                dcopy_ve[half] = last
                dcopy_ve["last"] = last
            tpose_pe[par % 3] = ctx.counts["pe"]
            dcopy_ve[h4] = last

        W4 = REP3 * 128

        def emit_q_prefetch(ctx, g, pos):
            nonlocal_it = it_box[0]
            if ctx.is_("sync") and nonlocal_it >= 2:
                ctx.h.wait_ge(sems["pe"], qslot_pe[nonlocal_it - 2])
            qsem = f"dq{nonlocal_it % 2}"
            dq = None
            for h4 in range(REP3):
                h = g * REP3 + h4
                i = ctx.h.dma_start(
                    q4b[nonlocal_it % 2][:, h4 * 256:(h4 + 1) * 256].rearrange("p (a w) -> p a w", a=2),
                    q_scr.ap()[2 * h:2 * h + 2, :, pos * 128:(pos + 1) * 128].rearrange("a p w -> p a w"),
                ) if ctx.is_("sync") else None
                dq = ctx.tick(qsem, 16, i)
            return dq

        def emit_av_evac(ctx, g, pos, itv, dvv, vbuf):
            nonlocal av_done_sc, prev_stat_ve
            EC = 2 * pos + 2
            ctx.wait("tensor", "sc", dcopy_ve["last"])
            if pos == 0:
                ctx.wait("tensor", "dv3", dvv)
            if ctx.is_("tensor") and av_done_sc:
                ctx.h.wait_ge(sems["sc"], av_done_sc)
            for dvh in range(2):
                for kc in range(EC):
                    i = ctx.h.matmul(ps_av[:, dvh * 512: dvh * 512 + W4],
                                     vbuf[:, kc * 256 + dvh * 128: kc * 256 + (dvh + 1) * 128],
                                     dT4[:, kc * W4: (kc + 1) * W4],
                                     start=(kc == 0), stop=(kc == EC - 1)) if ctx.is_("tensor") else None
                    pe = ctx.tick("pe", 1, i)
            ctx.wait("scalar", "pe", pe)
            if itv >= 2:
                ctx.wait("scalar", f"soa{itv % 2}", aout[itv - 2])
                ctx.wait("scalar", "ve", astat_ve[itv - 2])
                ctx.wait("scalar", "gp", gp_last[0])
            asb = astage[itv % 2]
            i = ctx.h.copy(asb[:].rearrange("p (a w) -> p a w", w=W4),
                           ps_av[:].rearrange("p (a w) -> p a w", w=512)[:, :, 0:W4]) if ctx.is_("scalar") else None
            av_done_sc = ctx.tick("sc", 1, i)
            ctx.wait("sync", "sc", av_done_sc)
            for h4 in range(REP3):
                h = g * REP3 + h4
                i = ctx.h.dma_start(
                    att_scr.ap()[h, pos, :, :, :].rearrange("a p w -> p a w"),
                    asb[:].rearrange("p (a b w) -> p a b w", a=2, b=REP3)[:, :, h4, :],
                ) if ctx.is_("sync") else None
                do = ctx.tick(f"soa{itv % 2}", 16, i)
            aout[itv] = do
            ctx.wait("gpsimd", "sc", av_done_sc)
            ctx.wait("vector", "sc", av_done_sc)
            for h4 in range(REP3):
                h = g * REP3 + h4
                aview = asb[:].rearrange("p (a b w) -> p a b w", a=2, b=REP3)[:, :, h4, :]
                ctx.wait("gpsimd", "ve", prev_stat_ve)
                i = ctx.h.tensor_mul(sq3[:].rearrange("p (a w) -> p a w", a=2), aview, aview) if ctx.is_("gpsimd") else None
                gp = ctx.tick("gp", 1, i)
                gp_last[0] = gp
                ctx.wait("vector", "ve", prev_stat_ve)
                i = ctx.h.reduce_sum(stat_tmp[:, 0:1], aview, axis=AX.XY) if ctx.is_("vector") else None
                v = ctx.tick("ve", 1, i)
                ctx.wait("vector", "gp", gp)
                i = ctx.h.reduce_sum(stat_tmp[:, 1:2], sq3[:].rearrange("p (a w) -> p a w", a=2),
                                     axis=AX.XY) if ctx.is_("vector") else None
                v = ctx.tick("ve", 1, i)
                ctx.wait("vector", "ve", v)
                i = ctx.h.tensor_add(statp_s[:, h:h + 1], statp_s[:, h:h + 1], stat_tmp[:, 0:1]) if ctx.is_("vector") else None
                ctx.tick("ve", 1, i)
                i = ctx.h.tensor_add(statp_q[:, h:h + 1], statp_q[:, h:h + 1], stat_tmp[:, 1:2]) if ctx.is_("vector") else None
                v = ctx.tick("ve", 1, i)
                prev_stat_ve = v
            astat_ve[itv] = v

        it_box = [0]

        def load_vg(ctx, g):
            i = ctx.h.dma_start(
                v_g[g % 2][:].rearrange("p (c w) -> p c w", c=NT),
                v_scr.ap()[:, :, g * 256:(g + 1) * 256].rearrange("c p w -> p c w"),
            ) if ctx.is_("sync") else None
            return ctx.tick("dv3", 16, i)

        dvv = load_vg(ctx, 0)
        hctr_base = [0]
        for g in range(KV):
            vbuf = v_g[g % 2]
            stream = [(pos, h4) for pos in range(NO) for h4 in range(REP3)]
            dq_by_pos = {}
            it0 = it_box[0]

            def emit_T_for(idx):
                pos_, h4_ = stream[idx]
                hc = hctr_base[0] + idx
                emit_transposes(ctx, g, pos_, h4_, hc, 2 * pos_ + 2)
                if h4_ == REP3 - 1:
                    itv = it0 + pos_
                    qslot_pe[itv] = ctx.counts["pe"]
                    emit_av_evac(ctx, g, pos_, itv, dvv, vbuf)

            for idx, (pos, h4) in enumerate(stream):
                it_box[0] = it0 + pos
                if h4 == 0:
                    dq_by_pos[pos] = emit_q_prefetch(ctx, g, pos)
                    if pos == NO - 1 and g + 1 < KV:
                        dvv_next[0] = load_vg(ctx, g + 1)
                hc = hctr_base[0] + idx
                EC = 2 * pos + 2
                emit_scores(ctx, g, pos, h4, hc, dq_by_pos[pos], EC, it0 + pos)
                emit_combine(ctx, g, pos, h4, hc, EC)
                if idx >= 2:
                    emit_T_for(idx - 2)
            emit_T_for(len(stream) - 2)
            emit_T_for(len(stream) - 1)
            it_box[0] = it0 + NO
            hctr_base[0] += len(stream)
            if g + 1 < KV:
                dvv = dvv_next[0]

    phase("p3", p3)
    if stop_after == "p3":
        _finish_early(nc, sems, counts)
        return nc
    st3.close()
    stk.close()

    st4 = contextlib.ExitStack()
    srow = sb("srow", (1, 4 * H), st=st4)
    brow = sb("brow", (128, 4 * H), st=st4)
    ps_st = ps("ps_st", (1, 2 * H), st=st4)
    ps_bc = ps("ps_bc", (128, 4 * H), st=st4)
    NTOT = float(T * 256)

    def p4(ctx):
        i = ctx.h.matmul(ps_st[:, 0:H], onesc[:], statp_s[:], start=True, stop=True) if ctx.is_("tensor") else None
        ctx.tick("pe", 1, i)
        i = ctx.h.matmul(ps_st[:, H:2 * H], onesc[:], statp_q[:], start=True, stop=True) if ctx.is_("tensor") else None
        pe = ctx.tick("pe", 1, i)
        ctx.wait("vector", "pe", pe)
        i = ctx.h.tensor_copy(srow[:, 0:2 * H], ps_st[:]) if ctx.is_("vector") else None
        v = ctx.tick("ve", 1, i)
        ctx.wait("sync", "ve", v)
        i = ctx.h.dma_start(cc_in.ap(), srow[:, 0:2 * H]) if ctx.is_("sync") else None
        do = ctx.tick("socc", 16, i)
        ctx.wait("gpsimd", "socc", do)
        i = ctx.h.collective_compute(
            "AllReduce", ALU.add, replica_groups=[[2 * i_, 2 * i_ + 1] for i_ in range(cfg["n_cores"] // 2)],
            ins=[cc_in.ap()], outs=[cc_out.ap()],
        ) if ctx.is_("gpsimd") else None
        cc = ctx.tick("cc", 1, i)
        ctx.wait("sync", "cc", cc)
        i = ctx.h.dma_start(srow[:, 0:2 * H], cc_out.ap()) if ctx.is_("sync") else None
        dv = ctx.tick("dcc", 16, i)
        ctx.wait("vector", "dcc", dv)
        i = ctx.h.tensor_single_scalar(srow[:, 2 * H:3 * H], srow[:, 0:H], -1.0 / NTOT, ALU.mult) if ctx.is_("vector") else None
        v = ctx.tick("ve", 1, i)
        ctx.wait("vector", "ve", v)
        i = ctx.h.tensor_single_scalar(srow[:, H:2 * H], srow[:, H:2 * H], 1.0 / NTOT, ALU.mult) if ctx.is_("vector") else None
        v = ctx.tick("ve", 1, i)
        ctx.wait("vector", "ve", v)
        i = ctx.h.tensor_mul(srow[:, 3 * H:4 * H], srow[:, 2 * H:3 * H], srow[:, 2 * H:3 * H]) if ctx.is_("vector") else None
        v = ctx.tick("ve", 1, i)
        ctx.wait("vector", "ve", v)
        i = ctx.h.tensor_sub(srow[:, H:2 * H], srow[:, H:2 * H], srow[:, 3 * H:4 * H]) if ctx.is_("vector") else None
        v = ctx.tick("ve", 1, i)
        ctx.wait("vector", "ve", v)
        i = ctx.h.tensor_single_scalar(srow[:, H:2 * H], srow[:, H:2 * H], float(EPS), ALU.add) if ctx.is_("vector") else None
        v = ctx.tick("ve", 1, i)
        ctx.wait("vector", "ve", v)
        ctx.wait("scalar", "ve", v)
        i = ctx.h.activation(srow[:, 3 * H:4 * H], srow[:, H:2 * H], AF.Ln) if ctx.is_("scalar") else None
        s = ctx.tick("sc", 1, i)
        ctx.wait("scalar", "sc", s)
        i = ctx.h.activation(srow[:, 0:H], srow[:, 3 * H:4 * H], AF.Exp, scale=-0.5) if ctx.is_("scalar") else None
        s = ctx.tick("sc", 1, i)
        ctx.wait("tensor", "sc", s)
        i = ctx.h.matmul(ps_bc[:, 0:H], onesr[:], srow[:, 0:H], start=True, stop=True) if ctx.is_("tensor") else None
        ctx.tick("pe", 1, i)
        i = ctx.h.matmul(ps_bc[:, H:2 * H], onesr[:], srow[:, 2 * H:3 * H], start=True, stop=True) if ctx.is_("tensor") else None
        pe = ctx.tick("pe", 1, i)
        ctx.wait("vector", "pe", pe)
        i = ctx.h.tensor_copy(brow[:, 0:2 * H], ps_bc[:, 0:2 * H]) if ctx.is_("vector") else None
        v = ctx.tick("ve", 1, i)
        ctx.wait("vector", "ve", v)
        for dvh in range(2):
            i = ctx.h.tensor_mul(scale_vec[:, dvh * H:(dvh + 1) * H], gnw_sb[:, dvh * H:(dvh + 1) * H],
                                 brow[:, 0:H]) if ctx.is_("vector") else None
            v = ctx.tick("ve", 1, i)
        ctx.wait("vector", "ve", v)
        i = ctx.h.tensor_single_scalar(scale_vec[:], scale_vec[:], float(OUT_FACTOR), ALU.mult) if ctx.is_("vector") else None
        v = ctx.tick("ve", 1, i)
        ctx.wait("vector", "ve", v)
        for dvh in range(2):
            i = ctx.h.tensor_mul(bias_vec[:, dvh * H:(dvh + 1) * H], scale_vec[:, dvh * H:(dvh + 1) * H],
                                 brow[:, H:2 * H]) if ctx.is_("vector") else None
            v = ctx.tick("ve", 1, i)
        ctx.wait("vector", "ve", v)
        i = ctx.h.scalar_tensor_tensor(bias_vec[:], gnb_sb[:], float(OUT_FACTOR), bias_vec[:],
                                       ALU.mult, ALU.add) if ctx.is_("vector") else None
        ctx.tick("ve", 1, i)

    phase("p4", p4)
    if stop_after == "p4":
        _finish_early(nc, sems, counts)
        return nc
    st4.close()

    st5 = contextlib.ExitStack()
    TG = min(512, TOWN)
    NTG = TOWN // TG
    GT5 = TG // 128
    out_acc = sb("out_acc", (128, DC * TG), st=st5)
    yh = [sb("yh0", (128, 4 * TG), st=st5), sb("yh1", (128, 4 * TG), st=st5)]
    wos = sb("wos0", (128, 4 * D), st=st5)
    x1g = sb("x1g", (128, 2 * D), st=st5)
    t5 = make_ffn_tiles(st5, 2)
    HP = (H + 1) // 2

    def p56(ctx):
        ffn_reset(t5)
        wo_used = {}
        ctx.wait("sync", "soa0", ctx.counts["soa0"])
        ctx.wait("sync", "soa1", ctx.counts["soa1"])
        ctx.wait("sync", "sox10", ctx.counts["sox10"])
        ctx.wait("sync", "sox11", ctx.counts["sox11"])
        prev_ve = prev_sc = 0
        ostore = {}
        sgi = 0
        for tg in range(NTG):
            pos0 = tg * GT5
            ctx.wait("vector", "pe", ctx.counts["pe"])
            i = ctx.h.memset(out_acc[:], 0.0) if ctx.is_("vector") else None
            v_acc = ctx.tick("ve", 1, i)
            for hp in range(HP):
                h0 = 2 * hp
                nheads = min(2, H - h0)
                nch = 2 * nheads
                ybuf = yh[hp % 2]
                prev = (tg, hp - 2) if hp >= 2 else ((tg - 1, HP - 2 + hp) if tg > 0 else None)
                if prev is not None and prev in wo_used:
                    ctx.wait("sync", "pe", wo_used[prev])
                ysem = f"dyh{hp % 2}"
                for j in range(nch):
                    h_, dvh = h0 + j // 2, j % 2
                    i = ctx.h.dma_start(
                        ybuf[:, j * TG:(j + 1) * TG].rearrange("p (o w) -> p o w", o=GT5),
                        att_scr.ap()[h_, pos0:pos0 + GT5, dvh, :, :].rearrange("o p w -> p o w"),
                    ) if ctx.is_("sync") else None
                    dv = ctx.tick(ysem, 16, i)
                prevw = (tg, hp - 1) if hp >= 1 else ((tg - 1, HP - 1) if tg > 0 else None)
                if prevw is not None and prevw in wo_used:
                    ctx.wait("sync", "pe", wo_used[prevw])
                i = ctx.h.dma_start(wos[:, 0:nch * D], wo_st.ap()[:, 4 * hp * D:(4 * hp + nch) * D]) if ctx.is_("sync") else None
                dw_ = ctx.tick("dwt0", 16, i)
                ctx.wait("vector", ysem, dv)
                for j in range(nch):
                    h_ = h0 + j // 2
                    dvh = j % 2
                    col = dvh * H + h_
                    i = ctx.h.tensor_scalar(ybuf[:, j * TG:(j + 1) * TG], ybuf[:, j * TG:(j + 1) * TG],
                                            scale_vec[:, col:col + 1], bias_vec[:, col:col + 1],
                                            ALU.mult, ALU.add) if ctx.is_("vector") else None
                    v_n = ctx.tick("ve", 1, i)
                ctx.wait("tensor", "ve", v_n)
                ctx.wait("tensor", "dwt0", dw_)
                ctx.wait("tensor", "ve", v_acc)
                psup = t5["ps_up"]
                slotguard = t5["state"]["psup_ve"]
                for m in range(DC):
                    slot = m % 2
                    pslot = psup[:, slot * 512:slot * 512 + TG]
                    if ctx.is_("tensor") and slot in slotguard:
                        ctx.h.wait_ge(sems["ve"], slotguard[slot])
                    if ctx.is_("tensor") and slot in t5["state"].get("psup_sc", {}):
                        ctx.h.wait_ge(sems["sc"], t5["state"]["psup_sc"][slot])
                    for j in range(nch):
                        i = ctx.h.matmul(pslot, wos[:, j * D + m * 128: j * D + (m + 1) * 128],
                                         ybuf[:, j * TG:(j + 1) * TG],
                                         start=(j == 0), stop=(j == nch - 1)) if ctx.is_("tensor") else None
                        pe = ctx.tick("pe", 1, i)
                    wo_used[(tg, hp)] = pe
                    ctx.wait("vector", "pe", pe)
                    i = ctx.h.tensor_add(out_acc[:, m * TG:(m + 1) * TG], out_acc[:, m * TG:(m + 1) * TG],
                                         pslot) if ctx.is_("vector") else None
                    slotguard[slot] = ctx.tick("ve", 1, i)
            ctx.wait("tensor", "ve", ctx.counts["ve"])
            for sg in range(GT5 // 2):
                ti0 = sg * 2
                c0 = pos0 + ti0
                if sgi > 0:
                    ctx.wait("sync", "ve", prev_ve)
                    ctx.wait("sync", "sc", prev_sc)
                    ctx.wait("scalar", "ve", prev_ve)
                    if (sgi - 1) in ostore:
                        ctx.wait("vector", "soo", ostore[sgi - 1])
                        ctx.wait("scalar", "soo", ostore[sgi - 1])
                i = ctx.h.dma_start(
                    x1g[:].rearrange("p (a w) -> p a w", a=2),
                    x1own_scr.ap()[c0:c0 + 2, :, :].rearrange("a p w -> p a w"),
                ) if ctx.is_("sync") else None
                dx1 = ctx.tick("dx1t", 16, i)
                X4v = t5["X4"][:].rearrange("p (a w) -> p a w", a=2)
                x1v = x1g[:].rearrange("p (a w) -> p a w", a=2)
                for m2 in range(DC // 2):
                    m = 2 * m2
                    ctx.wait("tensor", "ve", t5["state"]["pst_ve"])
                    for j in range(4):
                        ttl, mloc = j // 2, j % 2
                        i = ctx.h.transpose(t5["ps_t"][:, j * 128:(j + 1) * 128],
                                            out_acc[:, (m + mloc) * TG + (ti0 + ttl) * 128:
                                                    (m + mloc) * TG + (ti0 + ttl) * 128 + 128],
                                            ident[:]) if ctx.is_("tensor") else None
                        pe = ctx.tick("pe", 1, i)
                    ctx.wait("vector", "pe", pe)
                    ctx.wait("vector", "dx1t", dx1)
                    i = ctx.h.scalar_tensor_tensor(
                        X4v[:, :, m * 128:(m + 2) * 128],
                        t5["ps_t"][:].rearrange("p (a w) -> p a w", a=2), 1.0,
                        x1v[:, :, m * 128:(m + 2) * 128],
                        ALU.mult, ALU.add) if ctx.is_("vector") else None
                    v_x2 = ctx.tick("ve", 1, i)
                    t5["state"]["pst_ve"] = v_x2
                ctx.wait("vector", "pe", ctx.counts["pe"])
                def ydst5(tt, yb, y_ve):
                    emit_y_transposes(ctx, t5, tt, yb, y_ve, t5["YT4"][:], 256)
                emit_ln_group(ctx, t5, t5["X4"][:], t5["XR4"][:], ydst5, 2, src_ready=("ve", v_x2))
                v_o = emit_ffn_group(ctx, t5, 2)
                ctx.wait("sync", "ve", v_o)
                i = ctx.h.dma_start(
                    out_own.ap()[c0 * 128:(c0 + 2) * 128, :].rearrange("(a q) w -> q a w", a=2),
                    t5["XR4"][:].rearrange("p (a w) -> p a w", a=2),
                ) if ctx.is_("sync") else None
                ostore[sgi] = ctx.tick("soo", 16, i)
                prev_ve = ctx.counts["ve"]
                prev_sc = ctx.counts["sc"]
                sgi += 1
        ctx.wait("sync", "soo", ctx.counts["soo"])

    phase("p56", p56)
    st5.close()

    top.close()
    return nc



def kernel(**inputs):
    cfg = make_cfg()
    in_maps = make_core_inputs(cfg, inputs)
    nc = build_program(cfg)
    res = run_bass_kernel_spmd(nc, in_maps, core_ids=list(range(cfg["n_cores"])))
    return gather_output(cfg, res.results)


# version 14 (final)
# speedup vs baseline: 1.0223x; 1.0223x over previous
import contextlib
import math

import numpy as np

import concourse.bass as bass
import concourse.mybir as mybir
from concourse.bass_utils import run_bass_kernel_spmd

F32 = mybir.dt.float32
AF = mybir.ActivationFunctionType
ALU = mybir.AluOpType
AX = mybir.AxisListType

MASKVAL = -30000.0
EPS = 1e-5
LAM_INIT = float(0.8 - 0.6 * math.exp(-0.3 * 0.0))
OUT_FACTOR = 1.0 - LAM_INIT
ENGS = ("sync", "tensor", "vector", "scalar", "gpsimd")


def make_cfg(B=4, T=2048, D=2048, H=16, KV=4, n_cores=8):
    c = {}
    c["B"], c["T"], c["D"], c["H"], c["KV"] = B, T, D, H, KV
    c["TOWN"] = T // 2
    c["NT"] = T // 128
    c["NO"] = c["TOWN"] // 128
    c["DC"] = D // 128
    c["HQ"] = H * 256
    c["KC"] = KV * 256
    c["QC"] = c["HQ"] // 128
    c["KFC"] = c["KC"] // 128
    c["n_cores"] = n_cores
    c["scale"] = 1.0 / math.sqrt(128.0)
    return c



def _deinterleave_perm(ncols, block=128):
    p = []
    for b0 in range(0, ncols, block):
        p += [b0 + 2 * j for j in range(block // 2)]
        p += [b0 + 2 * j + 1 for j in range(block // 2)]
    return np.array(p, dtype=np.int64)


def _chunk_cols(mat, pdim=128):
    R, C = mat.shape
    return np.ascontiguousarray(
        mat.reshape(R // pdim, pdim, C).transpose(1, 0, 2).reshape(pdim, (R // pdim) * C)
    )


def make_core_inputs(cfg, inputs):
    B, T, D, H, KV = cfg["B"], cfg["T"], cfg["D"], cfg["H"], cfg["KV"]
    NT, NO, DC, HQ, KC = cfg["NT"], cfg["NO"], cfg["DC"], cfg["HQ"], cfg["KC"]
    TOWN = cfg["TOWN"]
    f32 = np.float32

    x = np.asarray(inputs["L0_attn_x"], dtype=f32)
    wq = np.asarray(inputs["L0_attn_Wq"], dtype=f32)
    wk = np.asarray(inputs["L0_attn_Wk"], dtype=f32)
    wv = np.asarray(inputs["L0_attn_Wv"], dtype=f32)
    wo = np.asarray(inputs["L0_attn_Wo"], dtype=f32)
    gn_w = np.asarray(inputs["L0_attn_gn_w"], dtype=f32)
    gn_b = np.asarray(inputs["L0_attn_gn_b"], dtype=f32)
    rw = np.asarray(inputs["L0_ffn_router_w"], dtype=f32)
    dw = np.asarray(inputs["L0_ffn_down_w"], dtype=f32)
    uw = np.asarray(inputs["L0_ffn_up_w"], dtype=f32)

    permq = _deinterleave_perm(HQ)
    permk = _deinterleave_perm(KC)
    wq_mv = _chunk_cols(wq[:, permq])
    wk_st = _chunk_cols(wk[:, permk])
    wv_sb = _chunk_cols(wv)
    wo_st = _chunk_cols(wo)
    dw_sbh = _chunk_cols(np.ascontiguousarray(dw.reshape(128, D).T))
    up_sbh = np.ascontiguousarray(uw.transpose(0, 2, 1).reshape(128, D))
    rt_sbh = _chunk_cols(np.ascontiguousarray(rw.T))
    gnw2 = np.ascontiguousarray(gn_w.reshape(H, 2, 128).transpose(2, 1, 0).reshape(128, 2 * H))
    gnb2 = np.ascontiguousarray(gn_b.reshape(H, 2, 128).transpose(2, 1, 0).reshape(128, 2 * H))

    lq1 = np.ascontiguousarray(np.asarray(inputs["L0_attn_lambda_q1"], f32).reshape(1, 128))
    lk1 = np.ascontiguousarray(np.asarray(inputs["L0_attn_lambda_k1"], f32).reshape(1, 128))
    lq2 = np.ascontiguousarray(np.asarray(inputs["L0_attn_lambda_q2"], f32).reshape(1, 128))
    lk2 = np.ascontiguousarray(np.asarray(inputs["L0_attn_lambda_k2"], f32).reshape(1, 128))

    ident = np.eye(128, dtype=f32)
    ones_col = np.ones((128, 1), dtype=f32)
    ones_row = np.ones((1, 128), dtype=f32)
    rep8 = np.zeros((8, 128), dtype=f32)
    for e in range(8):
        rep8[e, e * 16:(e + 1) * 16] = 1.0

    inv = 1.0 / (10000.0 ** (np.arange(0, 128, 2, dtype=np.float64) / 128.0))
    tpos = np.arange(T, dtype=np.float64)
    ang = tpos[:, None] * inv[None, :]
    cosk = np.ascontiguousarray(
        np.concatenate([np.cos(ang).T, np.sin(ang).T], axis=0).astype(f32))

    tri = np.where(np.arange(128)[None, :] > np.arange(128)[:, None], MASKVAL, 0.0).astype(f32)

    maps = []
    for core in range(cfg["n_cores"]):
        b = core // 2
        par = core % 2
        own_tiles = [2 * p + par for p in range(NO)]
        xb = x[b]
        x_all = np.ascontiguousarray(xb)
        x_own = np.ascontiguousarray(xb.reshape(NT, 128, D)[own_tiles].reshape(TOWN, D))
        own_pos = np.concatenate([np.arange(t0 * 128, t0 * 128 + 128) for t0 in own_tiles])
        a2 = own_pos.astype(np.float64)[:, None] * inv[None, :]
        cq = np.ascontiguousarray(
            np.concatenate([np.cos(a2).T, np.sin(a2).T], axis=0).astype(f32))
        maskt = np.zeros((128, 256), dtype=f32)
        if par == 1:
            maskt[:, 128:] = tri
        else:
            maskt[:, :128] = tri
            maskt[:, 128:] = MASKVAL
        maps.append({
            "x_all": x_all, "x_own": x_own,
            "wq_mv": wq_mv, "wk_st": wk_st, "wv_sb": wv_sb, "wo_st": wo_st,
            "dw_sbh": dw_sbh, "up_sbh": up_sbh, "rt_sbh": rt_sbh,
            "gnw2": gnw2, "gnb2": gnb2,
            "lq1": lq1, "lk1": lk1, "lq2": lq2, "lk2": lk2,
            "ident": ident, "ones_col": ones_col, "ones_row": ones_row,
            "rep8": rep8, "maskt": maskt,
            "cosq": cq, "cosk": cosk,
        })
    return maps


def gather_output(cfg, results):
    B, T, D, NO, NT = cfg["B"], cfg["T"], cfg["D"], cfg["NO"], cfg["NT"]
    out = np.empty((B, T, D), dtype=np.float32)
    for core, r in enumerate(results):
        b, par = core // 2, core % 2
        oo = np.asarray(r["out_own"]).reshape(NO, 128, D)
        for p in range(NO):
            t0 = (2 * p + par) * 128
            out[b, t0:t0 + 128, :] = oo[p]
    return out



class Ctx:

    def __init__(self, engname, handle, counts, sems):
        self.eng = engname
        self.h = handle
        self.counts = counts
        self.sems = sems

    def is_(self, name):
        return self.eng == name

    def tick(self, sem, n=1, inst=None):
        self.counts[sem] += n
        if inst is not None:
            inst.then_inc(self.sems[sem], n)
        return self.counts[sem]

    def wait(self, eng, sem, val):
        if self.eng == eng:
            self.h.wait_ge(self.sems[sem], val)


def _finish_early(nc, sems, counts):
    with nc.Block("fin") as blk:
        @blk.sync
        def _(sync):
            for s in ("soy", "sox10", "sox11", "soq0", "soq1", "sov0", "sov1", "soa0", "soa1", "soo", "socc"):
                if counts.get(s):
                    sync.wait_ge(sems[s], counts[s])


def build_program(cfg, debug=False, stop_after=None):
    B, T, D, H, KV = cfg["B"], cfg["T"], cfg["D"], cfg["H"], cfg["KV"]
    NT, NO, DC, HQ, KC = cfg["NT"], cfg["NO"], cfg["DC"], cfg["HQ"], cfg["KC"]
    TOWN, QC, KFC = cfg["TOWN"], cfg["QC"], cfg["KFC"]
    SC = cfg["scale"]
    H2 = 2 * H

    nc = bass.Bass("TRN2", target_bir_lowering=False)

    def din(name, shape):
        return nc.dram_tensor(name, list(shape), F32, kind="ExternalInput")

    x_all = din("x_all", (T, D))
    x_own = din("x_own", (TOWN, D))
    wq_mv = din("wq_mv", (128, DC * HQ))
    wk_st = din("wk_st", (128, DC * KC))
    wv_sb_d = din("wv_sb", (128, DC * KC))
    wo_st = din("wo_st", (128, QC * D))
    dw_d = din("dw_sbh", (128, DC * 128))
    up_d = din("up_sbh", (128, DC * 128))
    rt_d = din("rt_sbh", (128, DC * 8))
    gnw_d = din("gnw2", (128, H2))
    gnb_d = din("gnb2", (128, H2))
    lq1_d = din("lq1", (1, 128)); lk1_d = din("lk1", (1, 128))
    lq2_d = din("lq2", (1, 128)); lk2_d = din("lk2", (1, 128))
    ident_d = din("ident", (128, 128))
    onesc_d = din("ones_col", (128, 1))
    onesr_d = din("ones_row", (1, 128))
    rep8_d = din("rep8", (8, 128))
    maskt_d = din("maskt", (128, 256))
    cosq_d = din("cosq", (128, TOWN))
    cosk_d = din("cosk", (128, T))

    out_own = nc.dram_tensor("out_own", [TOWN, D], F32, kind="ExternalOutput")

    NG4 = NT // 4
    y2t_scr = nc.dram_tensor("y2t_scr", [NG4, 128, DC * 512], F32, kind="Internal")
    v_scr = nc.dram_tensor("v_scr", [NT, 128, KC], F32, kind="Internal")
    q_scr = nc.dram_tensor("q_scr", [QC, 128, TOWN], F32, kind="Internal")
    x1own_scr = nc.dram_tensor("x1own_scr", [NO, 128, D], F32, kind="Internal")
    att_scr = nc.dram_tensor("att_scr", [H, NO, 2, 128, 128], F32, kind="Internal")
    cc_in = nc.dram_tensor("cc_in", [1, H2], F32, kind="Internal")
    cc_out = nc.dram_tensor("cc_out", [1, H2], F32, kind="Internal")

    top = contextlib.ExitStack()
    sb = lambda name, shape, dt=F32, st=top: st.enter_context(nc.sbuf_tensor(name, list(shape), dt))
    ps = lambda name, shape, dt=F32, st=top: st.enter_context(nc.psum_tensor(name, list(shape), dt))

    sems = {}
    for s in ("din", "dx0", "dx1", "dwt0", "dwt1", "dy", "dq0", "dq1", "dv3",
              "dyh0", "dyh1", "dx1t", "dst4", "dcc",
              "soy", "sox10", "sox11", "soq0", "soq1", "sov0", "sov1", "soa0", "soa1", "soo", "socc",
              "pe", "ve", "sc", "cc", "gp"):
        sems[s] = top.enter_context(nc.semaphore(s))
    counts = {k: 0 for k in sems}

    STORE_SEMS = ("din", "dx0", "dx1", "dwt0", "dwt1", "dy", "dq0", "dq1", "dv3",
                  "dyh0", "dyh1", "dx1t", "dst4", "dcc",
                  "soy", "sox10", "sox11", "soq0", "soq1", "sov0", "sov1", "soa0", "soa1", "soo", "socc")

    def phase(name, body):
        finals = []
        entry = {s: counts[s] for s in STORE_SEMS}
        with nc.Block(name) as blk:
            for engname in ENGS:
                def mk(en):
                    def section(h):
                        if en in ("sync", "vector", "scalar"):
                            for s_, v_ in entry.items():
                                if v_:
                                    h.wait_ge(sems[s_], v_)
                        ctx = Ctx(en, h, dict(counts), sems)
                        body(ctx)
                        finals.append(ctx.counts)
                    return section
                getattr(blk, engname)(mk(engname))
        ref = finals[0]
        for f in finals[1:]:
            assert f == ref, f"phase {name}: engine counter divergence {ref} vs {f}"
        counts.update(ref)

    ident = sb("sb_ident", (128, 128))
    onesc = sb("sb_onesc", (128, 1))
    onesr = sb("sb_onesr", (1, 128))
    rep8 = sb("sb_rep8", (8, 128))
    maskt = sb("sb_maskt", (128, 256))
    dw_sb = sb("sb_dw_sb", (128, DC * 128))
    up_sb = sb("sb_up_sb", (128, DC * 128))
    rt_sb = sb("sb_rt_sb", (128, DC * 8))
    gnw_sb = sb("sb_gnw_sb", (128, H2)); gnb_sb = sb("sb_gnb_sb", (128, H2))
    lam_misc = sb("lam_misc", (1, 128))
    lamneg_bc = sb("lamneg_bc", (128, 1))
    statp_s = sb("statp_s", (128, H)); statp_q = sb("statp_q", (128, H))
    scale_vec = sb("scale_vec", (128, H2)); bias_vec = sb("bias_vec", (128, H2))

    st0 = contextlib.ExitStack()
    lrow = sb("lrow", (1, 128), st=st0)
    l2row = sb("l2row", (1, 128), st=st0)
    lvals = sb("lvals", (1, 8), st=st0)
    pb_p0 = ps("p0_bc", (128, 1), st=st0)

    def p0(ctx):
        if ctx.is_("scalar"):
            from concourse.hw_specs import get_activation_tables
            idx = list(get_activation_tables(nc.m.arch)).index("natural_log_exp_and_others")
            ctx.h.add_instruction(mybir.InstLoadActFuncSet(
                name=nc.get_next_instruction_name(), act_func_set_id=idx, ins=[], outs=[]))
        loads = [
            (ident, ident_d), (onesc, onesc_d), (onesr, onesr_d), (rep8, rep8_d),
            (maskt, maskt_d),
            (dw_sb, dw_d), (up_sb, up_d), (rt_sb, rt_d),
            (gnw_sb, gnw_d), (gnb_sb, gnb_d),
        ]
        for dst, src in loads:
            i = ctx.h.dma_start(dst[:], src.ap()) if ctx.is_("sync") else None
            ctx.tick("din", 16, i)
        for dst, src in ((lrow, lq1_d), (l2row, lk1_d)):
            i = ctx.h.dma_start(dst[:], src.ap()) if ctx.is_("sync") else None
            ctx.tick("din", 16, i)
        w = counts_val = ctx.counts["din"]
        ctx.wait("vector", "din", w)
        i = ctx.h.tensor_mul(lam_misc[:], lrow[:], l2row[:]) if ctx.is_("vector") else None
        v1 = ctx.tick("ve", 1, i)
        ctx.wait("vector", "ve", v1)
        i = ctx.h.reduce_sum(lvals[:, 0:1], lam_misc[:], axis=AX.X) if ctx.is_("vector") else None
        v2 = ctx.tick("ve", 1, i)
        ctx.wait("sync", "ve", v2)
        for dst, src in ((lrow, lq2_d), (l2row, lk2_d)):
            i = ctx.h.dma_start(dst[:], src.ap()) if ctx.is_("sync") else None
            ctx.tick("din", 16, i)
        w = ctx.counts["din"]
        ctx.wait("vector", "din", w)
        ctx.wait("vector", "ve", v2)
        i = ctx.h.tensor_mul(lam_misc[:], lrow[:], l2row[:]) if ctx.is_("vector") else None
        v3 = ctx.tick("ve", 1, i)
        ctx.wait("vector", "ve", v3)
        i = ctx.h.reduce_sum(lvals[:, 1:2], lam_misc[:], axis=AX.X) if ctx.is_("vector") else None
        v4 = ctx.tick("ve", 1, i)
        ctx.wait("scalar", "ve", v4)
        i = ctx.h.activation(lvals[:, 2:4], lvals[:, 0:2], AF.Exp) if ctx.is_("scalar") else None
        s1 = ctx.tick("sc", 1, i)
        ctx.wait("vector", "sc", s1)
        i = ctx.h.tensor_sub(lvals[:, 4:5], lvals[:, 3:4], lvals[:, 2:3]) if ctx.is_("vector") else None
        v5 = ctx.tick("ve", 1, i)
        ctx.wait("vector", "ve", v5)
        i = ctx.h.tensor_single_scalar(lvals[:, 5:6], lvals[:, 4:5], float(LAM_INIT), ALU.subtract) if ctx.is_("vector") else None
        v6 = ctx.tick("ve", 1, i)
        ctx.wait("tensor", "ve", v6)
        ctx.wait("tensor", "din", ctx.counts["din"])
        pb = pb_p0
        i = ctx.h.matmul(pb[:], onesr[:], lvals[:, 5:6], start=True, stop=True) if ctx.is_("tensor") else None
        t1 = ctx.tick("pe", 1, i)
        ctx.wait("vector", "pe", t1)
        i = ctx.h.tensor_copy(lamneg_bc[:], pb[:]) if ctx.is_("vector") else None
        ctx.tick("ve", 1, i)

    phase("p0", p0)
    if stop_after == "p0":
        _finish_early(nc, sems, counts)
        return nc
    st0.close()

    _ffn_seq = [0]

    def make_ffn_tiles(st, GS):
        t = {"GS": GS}
        n = _ffn_seq[0]; _ffn_seq[0] += 1
        pfx = f"ffn{n}_"
        W = GS * 128
        def sb2(name, shape):
            return sb(pfx + name, shape, st=st)
        def ps2(name, shape):
            return ps(pfx + name, shape, st=st)
        t["X4"] = sb2("X4", (128, GS * D))
        t["XR4"] = sb2("XR4", (128, GS * D))
        t["YT4"] = sb2("YT4", (128, DC * W))
        t["Y"] = [sb2("Y0", (128, D)), sb2("Y1", (128, D))]
        t["f_sb"] = [sb2("f0", (128, W)), sb2("f1", (128, W))]
        t["st4"] = sb2("st4", (128, 24))
        t["e8"] = sb2("e8", (128, 6 * 8 * GS + 64))
        t["g4"] = sb2("g4", (128, 8 * GS))
        t["lg4"] = sb2("lg4", (8, W))
        t["g84"] = sb2("g84", (8, W))
        t["a4"] = sb2("a4", (128, W))
        t["ag4"] = sb2("ag4", (128, W))
        t["ps_a"] = ps2("ps_a", (128, 512))
        t["ps_lg"] = ps2("ps_lg", (8, 512))
        t["ps_h"] = ps2("ps_h", (128, 512))
        t["ps_g"] = ps2("ps_g", (128, 512))
        t["ps_up"] = ps2("ps_up", (128, 1024))
        t["ps_t"] = ps2("ps_t", (128, 512))
        t["state"] = {}
        return t

    def ffn_reset(t):
        t["state"] = {"pst_ve": 0, "psa_ve": 0, "pst_sc": 0, "psa_sc": 0,
                      "yt_sc": 0, "psh_sc": 0,
                      "psg_ve": 0, "psup_ve": {}, "y_free": {}}

    def emit_ln_group(ctx, t, SRC4, DUMP4, YDST, ngroups, src_ready=None):
        GS = ngroups
        st = t["st4"]
        s4 = SRC4.rearrange("p (a w) -> p a w", a=GS)
        d4 = DUMP4.rearrange("p (a w) -> p a w", a=GS)
        if src_ready is not None:
            ctx.wait("vector", src_ready[0], src_ready[1])
            ctx.wait("scalar", src_ready[0], src_ready[1])
        i = ctx.h.reduce_sum(st[:, 0:GS], s4, axis=AX.X, negate=True) if ctx.is_("vector") else None
        v = ctx.tick("ve", 1, i)
        i = ctx.h.activation(DUMP4, SRC4, AF.Square) if ctx.is_("scalar") else None
        s = ctx.tick("sc", 1, i)
        ctx.wait("vector", "sc", s)
        i = ctx.h.reduce_sum(st[:, 4:4 + GS], d4, axis=AX.X) if ctx.is_("vector") else None
        v = ctx.tick("ve", 1, i)
        ctx.wait("vector", "ve", v)
        i = ctx.h.tensor_scalar(st[:, 0:GS], st[:, 0:GS], 1.0 / D, None, ALU.mult) if ctx.is_("vector") else None
        v = ctx.tick("ve", 1, i)
        ctx.wait("vector", "ve", v)
        i = ctx.h.tensor_scalar(st[:, 4:4 + GS], st[:, 4:4 + GS], 1.0 / D, None, ALU.mult) if ctx.is_("vector") else None
        v = ctx.tick("ve", 1, i)
        ctx.wait("vector", "ve", v)
        i = ctx.h.tensor_mul(st[:, 8:8 + GS], st[:, 0:GS], st[:, 0:GS]) if ctx.is_("vector") else None
        v = ctx.tick("ve", 1, i)
        ctx.wait("vector", "ve", v)
        i = ctx.h.tensor_sub(st[:, 8:8 + GS], st[:, 4:4 + GS], st[:, 8:8 + GS]) if ctx.is_("vector") else None
        v = ctx.tick("ve", 1, i)
        ctx.wait("vector", "ve", v)
        i = ctx.h.tensor_single_scalar(st[:, 8:8 + GS], st[:, 8:8 + GS], float(EPS), ALU.add) if ctx.is_("vector") else None
        v = ctx.tick("ve", 1, i)
        ctx.wait("scalar", "ve", v)
        i = ctx.h.activation(st[:, 12:12 + GS], st[:, 8:8 + GS], AF.Ln) if ctx.is_("scalar") else None
        s = ctx.tick("sc", 1, i)
        ctx.wait("scalar", "sc", s)
        i = ctx.h.activation(st[:, 12:12 + GS], st[:, 12:12 + GS], AF.Exp, scale=-0.5) if ctx.is_("scalar") else None
        s = ctx.tick("sc", 1, i)
        ctx.wait("vector", "sc", s)
        i = ctx.h.tensor_mul(st[:, 16:16 + GS], st[:, 0:GS], st[:, 12:12 + GS]) if ctx.is_("vector") else None
        v_stat = ctx.tick("ve", 1, i)
        for tt in range(GS):
            ctx.wait("vector", "ve", v_stat)
            yfree = t["state"]["y_free"].get(tt % 2)
            if yfree:
                ctx.wait("vector", "pe", yfree)
            yb = t["Y"][tt % 2]
            i = ctx.h.tensor_scalar(yb[:], SRC4[:, tt * D:(tt + 1) * D],
                                    st[:, 12 + tt:13 + tt], st[:, 16 + tt:17 + tt],
                                    ALU.mult, ALU.add) if ctx.is_("vector") else None
            v = ctx.tick("ve", 1, i)
            YDST(tt, yb, v)
        return v_stat

    def emit_y_transposes(ctx, t, tt, yb, y_ve, YT4AP, W, dst_fn=None):
        ctx.wait("tensor", "ve", y_ve)
        last_pe = None
        for q4_ in range((DC + 3) // 4):
            jmax = min(4, DC - q4_ * 4)
            buf = t["ps_a"] if q4_ % 2 == 0 else t["ps_t"]
            key = "psa_sc" if q4_ % 2 == 0 else "pst_sc"
            if ctx.is_("tensor") and t["state"].get(key):
                ctx.h.wait_ge(sems["sc"], t["state"][key])
            if ctx.is_("tensor") and t["state"][key.replace("_sc", "_ve")]:
                ctx.h.wait_ge(sems["ve"], t["state"][key.replace("_sc", "_ve")])
            for j in range(jmax):
                dc = q4_ * 4 + j
                i = ctx.h.transpose(buf[:, j * 128:(j + 1) * 128],
                                    yb[:, dc * 128:(dc + 1) * 128], ident[:]) if ctx.is_("tensor") else None
                pe = ctx.tick("pe", 1, i)
            last_pe = pe
            ctx.wait("scalar", "pe", pe)
            if dst_fn is not None:
                dst = dst_fn(q4_)
            else:
                dst = YT4AP[:, q4_ * 4 * W:(q4_ * 4 + jmax) * W].rearrange(
                    "p (a w) -> p a w", w=W)[:, :, tt * 128:(tt + 1) * 128]
            srcb = buf[:, 0:jmax * 128].rearrange("p (a w) -> p a w", w=128)
            i = ctx.h.copy(dst, srcb) if ctx.is_("scalar") else None
            t["state"][key] = ctx.tick("sc", 1, i)
            t["state"]["yt_sc"] = t["state"][key]
        t["state"]["y_free"][tt % 2] = last_pe
        return t["state"]["psa_ve"]

    def emit_ffn_group(ctx, t, GS, res_scale=0.5):
        W = GS * 128
        stt_state = t["state"]
        g4 = t["g4"]; e8 = t["e8"]; lg4 = t["lg4"]; g84 = t["g84"]
        ctx.wait("tensor", "sc", stt_state["yt_sc"])
        for dc in range(DC):
            i = ctx.h.matmul(t["ps_lg"][:, 0:W], rt_sb[:, dc * 8:(dc + 1) * 8],
                             t["YT4"][:, dc * W:(dc + 1) * W],
                             start=(dc == 0), stop=(dc == DC - 1)) if ctx.is_("tensor") else None
            pe = ctx.tick("pe", 1, i)
        ctx.wait("vector", "pe", pe)
        i = ctx.h.tensor_copy(lg4[:, 0:W], t["ps_lg"][:, 0:W]) if ctx.is_("vector") else None
        v = ctx.tick("ve", 1, i)
        if ctx.is_("tensor") and stt_state["psh_sc"]:
            ctx.h.wait_ge(sems["sc"], stt_state["psh_sc"])
        for dc in range(DC):
            i = ctx.h.matmul(t["ps_h"][:, 0:W], dw_sb[:, dc * 128:(dc + 1) * 128],
                             t["YT4"][:, dc * W:(dc + 1) * W],
                             start=(dc == 0), stop=(dc == DC - 1)) if ctx.is_("tensor") else None
            pe_h = ctx.tick("pe", 1, i)
        ctx.wait("tensor", "ve", v)
        ctx.wait("tensor", "ve", stt_state["pst_ve"])
        if stt_state.get("pst_sc"):
            ctx.wait("tensor", "sc", stt_state["pst_sc"])
        for tt in range(GS):
            i = ctx.h.transpose(t["ps_t"][:, tt * 8:(tt + 1) * 8],
                                lg4[:, tt * 128:(tt + 1) * 128], ident[0:8, 0:8]) if ctx.is_("tensor") else None
            pe = ctx.tick("pe", 1, i)
        ctx.wait("vector", "pe", pe)
        i = ctx.h.tensor_copy(g4[:, 0:8 * GS].rearrange("p (a w) -> p a w", w=8),
                              t["ps_t"][:, 0:8 * GS].rearrange("p (a w) -> p a w", w=8)) if ctx.is_("vector") else None
        v = ctx.tick("ve", 1, i)
        stt_state["pst_ve"] = v
        G8 = 8 * GS
        ctx.wait("scalar", "ve", v)
        i = ctx.h.activation(e8[:, 0:G8], g4[:, 0:G8], AF.Exp) if ctx.is_("scalar") else None
        s = ctx.tick("sc", 1, i)
        ctx.wait("vector", "sc", s)
        i = ctx.h.reduce_sum(e8[:, 4 * G8: 4 * G8 + GS], e8[:, 0:G8].rearrange("p (a w) -> p a w", w=8),
                             axis=AX.X) if ctx.is_("vector") else None
        v = ctx.tick("ve", 1, i)
        ctx.wait("vector", "ve", v)
        i = ctx.h.reciprocal(e8[:, 4 * G8 + 4: 4 * G8 + 4 + GS], e8[:, 4 * G8:4 * G8 + GS]) if ctx.is_("vector") else None
        v = ctx.tick("ve", 1, i)
        for tt in range(GS):
            ctx.wait("vector", "ve", v)
            i = ctx.h.tensor_scalar(e8[:, G8 + tt * 8: G8 + (tt + 1) * 8], e8[:, tt * 8:(tt + 1) * 8],
                                    e8[:, 4 * G8 + 4 + tt:4 * G8 + 5 + tt], None, ALU.mult) if ctx.is_("vector") else None
            vw = ctx.tick("ve", 1, i)
        ctx.wait("vector", "ve", vw)
        i = ctx.h.reduce_max(e8[:, 4 * G8 + 8:4 * G8 + 8 + GS],
                             e8[:, G8:2 * G8].rearrange("p (a w) -> p a w", w=8), axis=AX.X) if ctx.is_("vector") else None
        v = ctx.tick("ve", 1, i)
        for tt in range(GS):
            ctx.wait("vector", "ve", v)
            i = ctx.h.tensor_single_scalar(e8[:, 2 * G8 + tt * 8:2 * G8 + (tt + 1) * 8],
                                           e8[:, G8 + tt * 8:G8 + (tt + 1) * 8],
                                           e8[:, 4 * G8 + 8 + tt:4 * G8 + 9 + tt], ALU.is_ge) if ctx.is_("vector") else None
            vm = ctx.tick("ve", 1, i)
        ctx.wait("vector", "ve", vm)
        i = ctx.h.tensor_sub(e8[:, 3 * G8:4 * G8], e8[:, G8:2 * G8], e8[:, 2 * G8:3 * G8]) if ctx.is_("vector") else None
        v = ctx.tick("ve", 1, i)
        ctx.wait("vector", "ve", v)
        i = ctx.h.reduce_max(e8[:, 4 * G8 + 12:4 * G8 + 12 + GS],
                             e8[:, 3 * G8:4 * G8].rearrange("p (a w) -> p a w", w=8), axis=AX.X) if ctx.is_("vector") else None
        v = ctx.tick("ve", 1, i)
        for tt in range(GS):
            ctx.wait("vector", "ve", v)
            i = ctx.h.tensor_single_scalar(e8[:, 2 * G8 + tt * 8:2 * G8 + (tt + 1) * 8],
                                           e8[:, G8 + tt * 8:G8 + (tt + 1) * 8],
                                           e8[:, 4 * G8 + 12 + tt:4 * G8 + 13 + tt], ALU.is_ge) if ctx.is_("vector") else None
            vm = ctx.tick("ve", 1, i)
        ctx.wait("vector", "ve", vm)
        i = ctx.h.tensor_mul(g4[:, 0:G8], e8[:, G8:2 * G8], e8[:, 2 * G8:3 * G8]) if ctx.is_("vector") else None
        v_gate = ctx.tick("ve", 1, i)
        ctx.wait("tensor", "ve", v_gate)
        for tt in range(GS):
            i = ctx.h.transpose(t["ps_lg"][0:8, tt * 128:(tt + 1) * 128],
                                g4[:, tt * 8:(tt + 1) * 8], ident[:]) if ctx.is_("tensor") else None
            pe = ctx.tick("pe", 1, i)
        ctx.wait("vector", "pe", pe)
        i = ctx.h.tensor_copy(g84[:, 0:W], t["ps_lg"][0:8, 0:W]) if ctx.is_("vector") else None
        v = ctx.tick("ve", 1, i)
        ctx.wait("tensor", "ve", v)
        if ctx.is_("tensor") and stt_state["psg_ve"]:
            ctx.h.wait_ge(sems["ve"], stt_state["psg_ve"])
        i = ctx.h.matmul(t["ps_g"][:, 0:W], rep8[:], g84[:, 0:W], start=True, stop=True) if ctx.is_("tensor") else None
        pe_g = ctx.tick("pe", 1, i)
        ctx.wait("scalar", "pe", pe_h)
        i = ctx.h.activation(t["a4"][:, 0:W], t["ps_h"][:, 0:W], AF.Exp, scale=-1.0) if ctx.is_("scalar") else None
        s = ctx.tick("sc", 1, i)
        stt_state["psh_sc"] = s
        ctx.wait("vector", "sc", s)
        i = ctx.h.tensor_single_scalar(t["a4"][:, 0:W], t["a4"][:, 0:W], 1.0, ALU.add) if ctx.is_("vector") else None
        v = ctx.tick("ve", 1, i)
        ctx.wait("vector", "ve", v)
        i = ctx.h.reciprocal(t["a4"][:, 0:W], t["a4"][:, 0:W]) if ctx.is_("vector") else None
        v = ctx.tick("ve", 1, i)
        ctx.wait("vector", "ve", v)
        i = ctx.h.tensor_mul(t["a4"][:, 0:W], t["a4"][:, 0:W], t["ps_h"][:, 0:W]) if ctx.is_("vector") else None
        v = ctx.tick("ve", 1, i)
        ctx.wait("vector", "ve", v)
        ctx.wait("vector", "pe", pe_g)
        i = ctx.h.tensor_mul(t["ag4"][:, 0:W], t["a4"][:, 0:W], t["ps_g"][:, 0:W]) if ctx.is_("vector") else None
        v_ag = ctx.tick("ve", 1, i)
        stt_state["psg_ve"] = v_ag
        X4v = t["X4"][:].rearrange("p (a w) -> p a w", a=GS)
        XR4v = t["XR4"][:].rearrange("p (a w) -> p a w", a=GS)
        ctx.wait("tensor", "ve", v_ag)
        evac = t["state"].setdefault("psup_sc", {})
        evac_by_m = {}
        v_last = v_ag
        for m in range(DC + 1):
            if m < DC:
                if ctx.is_("tensor") and (m % 2) in evac:
                    ctx.h.wait_ge(sems["sc"], evac[m % 2])
                if ctx.is_("tensor") and (m % 2) in t["state"]["psup_ve"]:
                    ctx.h.wait_ge(sems["ve"], t["state"]["psup_ve"][m % 2])
                i = ctx.h.matmul(t["ps_up"][:, (m % 2) * 512:(m % 2) * 512 + W],
                                 up_sb[:, m * 128:(m + 1) * 128], t["ag4"][:, 0:W],
                                 start=True, stop=True) if ctx.is_("tensor") else None
                pe_up = ctx.tick("pe", 1, i)
                ctx.wait("scalar", "pe", pe_up)
                i = ctx.h.copy(t["f_sb"][m % 2][:, 0:W],
                               t["ps_up"][:, (m % 2) * 512:(m % 2) * 512 + W]) if ctx.is_("scalar") else None
                evac[m % 2] = ctx.tick("sc", 1, i)
                evac_by_m[m] = evac[m % 2]
            if m >= 1:
                mm = m - 1
                tb = t["ps_t"] if mm % 2 == 0 else t["ps_a"]
                tkey = "pst_ve" if mm % 2 == 0 else "psa_ve"
                tskey = "pst_sc" if mm % 2 == 0 else "psa_sc"
                ctx.wait("tensor", "sc", evac_by_m[mm])
                ctx.wait("tensor", "ve", stt_state[tkey])
                if stt_state.get(tskey):
                    ctx.wait("tensor", "sc", stt_state[tskey])
                for j in range(GS):
                    i = ctx.h.transpose(tb[:, j * 128:(j + 1) * 128],
                                        t["f_sb"][mm % 2][:, j * 128:(j + 1) * 128], ident[:]) if ctx.is_("tensor") else None
                    pe_t = ctx.tick("pe", 1, i)
                ctx.wait("vector", "pe", pe_t)
                i = ctx.h.scalar_tensor_tensor(XR4v[:, :, mm * 128:(mm + 1) * 128],
                                               tb[:, 0:W].rearrange("p (a w) -> p a w", w=128),
                                               float(res_scale),
                                               X4v[:, :, mm * 128:(mm + 1) * 128],
                                               ALU.mult, ALU.add) if ctx.is_("vector") else None
                v_last = ctx.tick("ve", 1, i)
                stt_state[tkey] = v_last
        return v_last

    st1 = contextlib.ExitStack()
    t1 = make_ffn_tiles(st1, 4)

    def p1a(ctx):
        ffn_reset(t1)
        prev_ve = prev_sc = 0
        ystore = {}
        for grp in range(NG4):
            if grp > 0:
                ctx.wait("sync", "ve", prev_ve)
                ctx.wait("sync", "sc", prev_sc)
                ctx.wait("scalar", "ve", prev_ve)
                ctx.wait("vector", "soy", ystore[grp - 1])
            i = ctx.h.dma_start(
                t1["X4"][:].rearrange("p (a w) -> p a w", a=4),
                x_all.ap()[grp * 512:(grp + 1) * 512, :].rearrange("(a q) w -> q a w", a=4),
            ) if ctx.is_("sync") else None
            dv = ctx.tick("dx0", 16, i)

            def ydst1(tt, yb, y_ve):
                emit_y_transposes(ctx, t1, tt, yb, y_ve, t1["YT4"][:], 512)
            emit_ln_group(ctx, t1, t1["X4"][:], t1["XR4"][:], ydst1, 4, src_ready=("dx0", dv))
            v_xr = emit_ffn_group(ctx, t1, 4)
            ctx.wait("vector", "pe", ctx.counts["pe"])
            def ydst2(tt, yb, y_ve):
                emit_y_transposes(ctx, t1, tt, yb, y_ve, t1["YT4"][:], 512)
            emit_ln_group(ctx, t1, t1["XR4"][:], t1["X4"][:], ydst2, 4, src_ready=("ve", v_xr))
            ctx.wait("sync", "sc", t1["state"]["yt_sc"])
            i = ctx.h.dma_start(y2t_scr.ap()[grp, :, :], t1["YT4"][:]) if ctx.is_("sync") else None
            ystore[grp] = ctx.tick("soy", 16, i)
            prev_ve = ctx.counts["ve"]
            prev_sc = ctx.counts["sc"]

    phase("p1a", p1a)
    st1.close()

    st2 = contextlib.ExitStack()
    y2ownT = sb("y2ownT", (128, DC * TOWN), st=st2)

    st2a = contextlib.ExitStack()
    t2 = make_ffn_tiles(st2a, 2)
    NOG = NO // 2

    def p2a(ctx):
        ffn_reset(t2)
        prev_ve = prev_sc = 0
        x1store = {}
        for grp in range(NOG):
            if grp > 0:
                ctx.wait("sync", "ve", prev_ve)
                ctx.wait("sync", "sc", prev_sc)
                ctx.wait("scalar", "ve", prev_ve)
                sx = f"sox1{(grp - 1) % 2}"
                ctx.wait("scalar", sx, x1store[grp - 1])
                ctx.wait("vector", sx, x1store[grp - 1])
            i = ctx.h.dma_start(
                t2["X4"][:].rearrange("p (a w) -> p a w", a=2),
                x_own.ap()[grp * 256:(grp + 1) * 256, :].rearrange("(a q) w -> q a w", a=2),
            ) if ctx.is_("sync") else None
            dv = ctx.tick("dx0", 16, i)

            def ydst1(tt, yb, y_ve):
                emit_y_transposes(ctx, t2, tt, yb, y_ve, t2["YT4"][:], 256)
            emit_ln_group(ctx, t2, t2["X4"][:], t2["XR4"][:], ydst1, 2, src_ready=("dx0", dv))
            v_xr = emit_ffn_group(ctx, t2, 2)
            ctx.wait("sync", "ve", v_xr)
            i = ctx.h.dma_start(
                x1own_scr.ap()[grp * 2:(grp + 1) * 2, :, :].rearrange("a p w -> p a w"),
                t2["XR4"][:].rearrange("p (a w) -> p a w", a=2),
            ) if ctx.is_("sync") else None
            x1store[grp] = ctx.tick(f"sox1{grp % 2}", 16, i)
            ctx.wait("vector", "pe", ctx.counts["pe"])
            def ydst2(tt, yb, y_ve, _g=grp):
                c = _g * 2 + tt
                def dst_into_y2own(q4_):
                    jm = min(4, DC - q4_ * 4)
                    return y2ownT[:, q4_ * 4 * TOWN:(q4_ * 4 + jm) * TOWN].rearrange(
                        "p (a w) -> p a w", w=TOWN)[:, :, c * 128:(c + 1) * 128]
                emit_y_transposes(ctx, t2, tt, yb, y_ve, None, 256, dst_fn=dst_into_y2own)
            emit_ln_group(ctx, t2, t2["XR4"][:], t2["X4"][:], ydst2, 2, src_ready=("ve", v_xr))
            prev_ve = ctx.counts["ve"]
            prev_sc = ctx.counts["sc"]

    phase("p2a", p2a)
    st2a.close()

    st2b = contextlib.ExitStack()
    cossin_q = sb("sb_cosq", (128, TOWN), st=st2b)
    cosq = cossin_q[0:64, :]; sinq = cossin_q[64:128, :]
    NQG = QC // 4
    wqg = [sb("wqg0", (128, DC * 512), st=st2b), sb("wqg1", (128, DC * 512), st=st2b)]
    qst = [sb("qst0", (128, TOWN), st=st2b), sb("qst1", (128, TOWN), st=st2b)]
    rqtmp = sb("rqtmp", (64, TOWN), st=st2b)
    rqtmp2 = sb("rqtmp2", (64, TOWN), st=st2b)
    ps_q = [ps("ps_q0", (128, TOWN), st=st2b), ps("ps_q1", (128, TOWN), st=st2b)]
    NTP = (TOWN + 511) // 512

    def p2b(ctx):
        rope_done = {}
        store_done = {}
        wq_done = {}
        i = ctx.h.dma_start(cossin_q[:], cosq_d.ap()) if ctx.is_("sync") else None
        dcq = ctx.tick("din", 16, i)
        ctx.wait("vector", "din", dcq)
        for qg in range(NQG):
            wbuf = wqg[qg % 2]
            if ctx.is_("sync") and qg >= 2:
                ctx.h.wait_ge(sems["pe"], wq_done[qg - 2])
            wsem = f"dwt{qg % 2}"
            i = ctx.h.dma_start(
                wbuf[:].rearrange("p (dc w) -> p dc w", dc=DC),
                wq_mv.ap().rearrange("p (dc q) -> p dc q", dc=DC)[:, :, qg * 512:(qg + 1) * 512],
            ) if ctx.is_("sync") else None
            dv = ctx.tick(wsem, 16, i)
            ctx.wait("tensor", wsem, dv)
            for qc4 in range(4):
                qc = qg * 4 + qc4
                psq = ps_q[qc % 2]
                if ctx.is_("tensor") and qc >= 2:
                    ctx.h.wait_ge(sems["ve"], rope_done[qc - 2])
                for dc in range(DC):
                    for tp in range(NTP):
                        t0_ = tp * 512
                        tl = min(512, TOWN - t0_)
                        i = ctx.h.matmul(psq[:, t0_:t0_ + tl],
                                         wbuf[:, dc * 512 + qc4 * 128: dc * 512 + (qc4 + 1) * 128],
                                         y2ownT[:, dc * TOWN + t0_: dc * TOWN + t0_ + tl],
                                         start=(dc == 0), stop=(dc == DC - 1)) if ctx.is_("tensor") else None
                        pe = ctx.tick("pe", 1, i)
                wq_done[qg] = pe
                ctx.wait("vector", "pe", pe)
                qb = qst[qc % 2]
                if qc >= 2:
                    ctx.wait("vector", f"soq{qc % 2}", store_done[qc - 2])
                if qc > 0:
                    ctx.wait("vector", "ve", rope_done[qc - 1])
                ev = psq[0:64, :]; od = psq[64:128, :]
                i = ctx.h.tensor_mul(rqtmp[:], ev, cosq) if ctx.is_("vector") else None
                ctx.tick("ve", 1, i)
                i = ctx.h.tensor_mul(qb[0:64, :], od, sinq) if ctx.is_("vector") else None
                v = ctx.tick("ve", 1, i)
                ctx.wait("vector", "ve", v)
                i = ctx.h.tensor_sub(qb[0:64, :], rqtmp[:], qb[0:64, :]) if ctx.is_("vector") else None
                v = ctx.tick("ve", 1, i)
                ctx.wait("vector", "ve", v)
                i = ctx.h.tensor_mul(rqtmp[:], ev, sinq) if ctx.is_("vector") else None
                ctx.tick("ve", 1, i)
                i = ctx.h.tensor_mul(rqtmp2[:], od, cosq) if ctx.is_("vector") else None
                v = ctx.tick("ve", 1, i)
                ctx.wait("vector", "ve", v)
                i = ctx.h.tensor_add(qb[64:128, :], rqtmp[:], rqtmp2[:]) if ctx.is_("vector") else None
                v = ctx.tick("ve", 1, i)
                rope_done[qc] = v
                ctx.wait("sync", "ve", v)
                i = ctx.h.dma_start(q_scr.ap()[qc, :, :], qb[:]) if ctx.is_("sync") else None
                store_done[qc] = ctx.tick(f"soq{qc % 2}", 16, i)

    phase("p2b", p2b)
    if stop_after == "p2b":
        _finish_early(nc, sems, counts)
        return nc
    st2b.close()
    st2.close()

    stk = contextlib.ExitStack()
    k_sb = sb("k_sb", (128, KV * 2 * T), st=stk)

    st1b = contextlib.ExitStack()
    cossin_k = sb("sb_cosk", (128, T), st=st1b)
    wk_sbt = sb("wk_sbt", (128, DC * KC), st=st1b)
    y2g = sb("y2g", (128, DC * 512), st=st1b)
    rktmp = sb("rktmp", (64, 1024), st=st1b)
    ps_k = ps("ps_k", (128, 512), st=st1b)
    GT = 512 // 128
    NG = T // 512

    def p1b(ctx):
        soy_fin = ctx.counts["soy"]
        ctx.wait("sync", "soy", soy_fin)
        i = ctx.h.dma_start(cossin_k[:], cosk_d.ap()) if ctx.is_("sync") else None
        dck = ctx.tick("din", 16, i)
        ctx.wait("vector", "din", dck)
        i = ctx.h.dma_start(wk_sbt[:], wk_st.ap()) if ctx.is_("sync") else None
        dw_ = ctx.tick("dwt0", 16, i)
        for grp in range(NG):
            i = ctx.h.dma_start(y2g[:], y2t_scr.ap()[grp, :, :]) if ctx.is_("sync") else None
            dv = ctx.tick("dy", 16, i)
            ctx.wait("tensor", "dy", dv)
            if grp == 0:
                ctx.wait("tensor", "dwt0", dw_)
            for fc in range(KFC):
                if ctx.is_("tensor") and (grp, fc) != (0, 0):
                    ctx.h.wait_ge(sems["ve"], v_prev)
                for dc in range(DC):
                    i = ctx.h.matmul(ps_k[:], wk_sbt[:, dc * KC + fc * 128: dc * KC + (fc + 1) * 128],
                                     y2g[:, dc * 512:(dc + 1) * 512],
                                     start=(dc == 0), stop=(dc == DC - 1)) if ctx.is_("tensor") else None
                    pe = ctx.tick("pe", 1, i)
                ctx.wait("vector", "pe", pe)
                base = fc * T + grp * 512
                ct = cossin_k[0:64, grp * 512:grp * 512 + 512]
                stt = cossin_k[64:128, grp * 512:grp * 512 + 512]
                ev = ps_k[0:64, :]; od = ps_k[64:128, :]
                i = ctx.h.tensor_mul(rktmp[:, 0:512], ev, ct) if ctx.is_("vector") else None
                ctx.tick("ve", 1, i)
                i = ctx.h.tensor_mul(rktmp[:, 512:1024], od, stt) if ctx.is_("vector") else None
                v = ctx.tick("ve", 1, i)
                ctx.wait("vector", "ve", v)
                i = ctx.h.tensor_sub(k_sb[0:64, base:base + 512], rktmp[:, 0:512], rktmp[:, 512:1024]) if ctx.is_("vector") else None
                v = ctx.tick("ve", 1, i)
                ctx.wait("vector", "ve", v)
                i = ctx.h.tensor_mul(rktmp[:, 0:512], ev, stt) if ctx.is_("vector") else None
                ctx.tick("ve", 1, i)
                i = ctx.h.tensor_mul(rktmp[:, 512:1024], od, ct) if ctx.is_("vector") else None
                v = ctx.tick("ve", 1, i)
                ctx.wait("vector", "ve", v)
                i = ctx.h.tensor_add(k_sb[64:128, base:base + 512], rktmp[:, 0:512], rktmp[:, 512:1024]) if ctx.is_("vector") else None
                v_prev = ctx.tick("ve", 1, i)
            if ctx.is_("sync"):
                ctx.h.wait_ge(sems["pe"], pe)

    phase("p1b", p1b)
    if stop_after == "p1b":
        _finish_early(nc, sems, counts)
        return nc
    st1b.close()

    st1c = contextlib.ExitStack()
    wv_s = sb("wv_s", (128, DC * KC), st=st1c)
    y2g2 = sb("y2g2", (128, DC * 512), st=st1c)
    vstage = [sb("vstage0", (128, KC), st=st1c), sb("vstage1", (128, KC), st=st1c)]
    ps_v = ps("ps_v", (128, 2048), st=st1c)
    assert KC <= 1024

    def p1c(ctx):
        i = ctx.h.dma_start(wv_s[:], wv_sb_d.ap()) if ctx.is_("sync") else None
        dw_ = ctx.tick("dwt0", 16, i)
        NVP = (KC + 511) // 512
        evac_sc = {}
        vst = {}
        pe = 0
        for grp in range(NG4):
            if grp > 0:
                ctx.wait("sync", "pe", pe)
            i = ctx.h.dma_start(y2g2[:], y2t_scr.ap()[grp, :, :]) if ctx.is_("sync") else None
            dv = ctx.tick("dy", 16, i)
            ctx.wait("tensor", "dy", dv)
            if grp == 0:
                ctx.wait("tensor", "dwt0", dw_)
            for tt in range(4):
                c = grp * 4 + tt
                slot = c % 2
                pbase = slot * 1024
                if ctx.is_("tensor") and (c - 2) in evac_sc:
                    ctx.h.wait_ge(sems["sc"], evac_sc[c - 2])
                for dc in range(DC):
                    for vp in range(NVP):
                        w0 = vp * 512
                        wlen = min(512, KC - w0)
                        i = ctx.h.matmul(ps_v[:, pbase + w0: pbase + w0 + wlen],
                                         y2g2[:, dc * 512 + tt * 128: dc * 512 + (tt + 1) * 128],
                                         wv_s[:, dc * KC + w0: dc * KC + w0 + wlen],
                                         start=(dc == 0), stop=(dc == DC - 1)) if ctx.is_("tensor") else None
                        pe = ctx.tick("pe", 1, i)
                ctx.wait("scalar", "pe", pe)
                if (c - 2) in vst:
                    ctx.wait("scalar", f"sov{slot}", vst[c - 2])
                i = ctx.h.copy(vstage[slot][:], ps_v[:, pbase:pbase + KC]) if ctx.is_("scalar") else None
                evac_sc[c] = ctx.tick("sc", 1, i)
                ctx.wait("sync", "sc", evac_sc[c])
                i = ctx.h.dma_start(v_scr.ap()[c, :, :], vstage[slot][:]) if ctx.is_("sync") else None
                vst[c] = ctx.tick(f"sov{slot}", 16, i)

    phase("p1c", p1c)
    if stop_after == "p1c":
        _finish_early(nc, sems, counts)
        return nc
    st1c.close()

    st3 = contextlib.ExitStack()
    REP3 = H // KV
    v_g = [sb("v_g0", (128, NT * 256), st=st3), sb("v_g1", (128, NT * 256), st=st3)]
    q4b = [sb("q4b0", (128, REP3 * 256), st=st3), sb("q4b1", (128, REP3 * 256), st=st3)]
    e1_sb = [sb("e1a", (128, T), st=st3), sb("e1b", (128, T), st=st3), sb("e1c", (128, T), st=st3)]
    e2_sb = [sb("e2a", (128, T), st=st3), sb("e2b", (128, T), st=st3)]
    dT4 = sb("dT4", (128, NT * REP3 * 128), st=st3)
    nrow = sb("nrow3", (128, 32), st=st3)
    astage = [sb("astage0", (128, REP3 * 256), st=st3), sb("astage1", (128, REP3 * 256), st=st3)]
    sq3 = sb("sq3", (128, 256), st=st3)
    stat_tmp = sb("stat_tmp", (128, 4), st=st3)
    ps_seg = ps("ps_seg", (128, 4 * 512), st=st3)
    ps_t3 = ps("ps_t3", (128, 1024), st=st3)
    ps_av = ps("ps_av", (128, 1024), st=st3)

    def p3(ctx):
        i = ctx.h.memset(statp_s[:], 0.0) if ctx.is_("vector") else None
        ctx.tick("ve", 1, i)
        i = ctx.h.memset(statp_q[:], 0.0) if ctx.is_("vector") else None
        v = ctx.tick("ve", 1, i)
        ctx.wait("gpsimd", "ve", v)
        ctx.wait("sync", "sov0", ctx.counts["sov0"])
        ctx.wait("sync", "sov1", ctx.counts["sov1"])
        ctx.wait("sync", "soq0", ctx.counts["soq0"])
        ctx.wait("sync", "soq1", ctx.counts["soq1"])
        slot_sc = [0, 0, 0, 0]
        seg_i = [0]
        tpose_pe = {}
        dcopy_ve = {}
        d_ve = {}
        exp_sc = {}
        av_done_sc = 0
        astat_ve = {}
        prev_stat_ve = 0
        dvv_next = [0]
        gp_last = [0]
        qslot_pe = {}
        aout = {}
        it = 0

        def emit_scores(ctx, g, pos, h4, hctr, qsem_val, EC, itv):
            E = EC * 128
            par = hctr
            e1, e2 = e1_sb[par % 3], e2_sb[par % 2]
            for pol in range(2):
                esb = e1 if pol == 0 else e2
                kbase = (g * 2 + pol) * T
                nseg = (E + 511) // 512
                seg_order = [nseg - 1] + list(range(nseg - 1))
                for seg in seg_order:
                    s0 = seg * 512
                    slen = min(512, E - s0)
                    sl = seg_i[0] % 4
                    seg_i[0] += 1
                    pslot = ps_seg[:, sl * 512: sl * 512 + slen]
                    if ctx.is_("tensor"):
                        if slot_sc[sl]:
                            ctx.h.wait_ge(sems["sc"], slot_sc[sl])
                        ctx.h.wait_ge(sems[f"dq{itv % 2}"], qsem_val)
                    i = ctx.h.matmul(pslot, q4b[itv % 2][:, h4 * 256 + pol * 128: h4 * 256 + (pol + 1) * 128],
                                     k_sb[:, kbase + s0: kbase + s0 + slen],
                                     start=True, stop=True) if ctx.is_("tensor") else None
                    pe = ctx.tick("pe", 1, i)
                    is_last = (seg == nseg - 1)
                    if is_last:
                        m0 = slen - 256
                        ctx.wait("vector", "pe", pe)
                        i = ctx.h.tensor_add(pslot[:, m0:m0 + 256], pslot[:, m0:m0 + 256], maskt[:]) if ctx.is_("vector") else None
                        vm = ctx.tick("ve", 1, i)
                        ctx.wait("scalar", "ve", vm)
                    else:
                        ctx.wait("scalar", "pe", pe)
                    if seg == 0 and pol == 0:
                        prev = tpose_pe.get(par % 3)
                        if prev:
                            ctx.wait("scalar", "pe", prev)
                        if (hctr - 2) in d_ve:
                            ctx.wait("scalar", "ve", d_ve[hctr - 2])
                    acol = (par % 2) * 16 + pol * 4 + seg
                    i = ctx.h.activation(esb[:, s0:s0 + slen], pslot, AF.Exp,
                                         scale=float(SC), accum_out=nrow[:, acol:acol + 1]) if ctx.is_("scalar") else None
                    se = ctx.tick("sc", 1, i)
                    slot_sc[sl] = se
                exp_sc[(par % 2, pol)] = se

        def emit_combine(ctx, g, pos, h4, hctr, EC):
            E = EC * 128
            par = hctr
            e1, e2 = e1_sb[par % 3], e2_sb[par % 2]
            nb = (par % 2) * 16
            ctx.wait("vector", "sc", exp_sc[(par % 2, 1)])
            nseg = (E + 511) // 512
            i = ctx.h.reduce_sum(nrow[:, nb + 8:nb + 9], nrow[:, nb:nb + nseg], axis=AX.X) if ctx.is_("vector") else None
            v = ctx.tick("ve", 1, i)
            i = ctx.h.reduce_sum(nrow[:, nb + 9:nb + 10], nrow[:, nb + 4:nb + 4 + nseg], axis=AX.X) if ctx.is_("vector") else None
            v = ctx.tick("ve", 1, i)
            ctx.wait("vector", "ve", v)
            i = ctx.h.reciprocal(nrow[:, nb + 10:nb + 12], nrow[:, nb + 8:nb + 10]) if ctx.is_("vector") else None
            v = ctx.tick("ve", 1, i)
            ctx.wait("vector", "ve", v)
            i = ctx.h.tensor_mul(nrow[:, nb + 12:nb + 13], nrow[:, nb + 11:nb + 12], lamneg_bc[:]) if ctx.is_("vector") else None
            v = ctx.tick("ve", 1, i)
            ctx.wait("vector", "ve", v)
            i = ctx.h.tensor_scalar(e1[:, 0:E], e1[:, 0:E], nrow[:, nb + 10:nb + 11], None, ALU.mult) if ctx.is_("vector") else None
            v = ctx.tick("ve", 1, i)
            ctx.wait("vector", "ve", v)
            i = ctx.h.scalar_tensor_tensor(e1[:, 0:E], e2[:, 0:E], nrow[:, nb + 12:nb + 13],
                                           e1[:, 0:E], ALU.mult, ALU.add) if ctx.is_("vector") else None
            d_ve[hctr] = ctx.tick("ve", 1, i)

        def emit_transposes(ctx, g, pos, h4, hctr, EC):
            par = hctr
            e1 = e1_sb[par % 3]
            ctx.wait("tensor", "ve", d_ve[hctr])
            last = None
            for q4_ in range((EC + 3) // 4):
                jmax = min(4, EC - q4_ * 4)
                half = q4_ % 2
                if ctx.is_("tensor"):
                    prev = dcopy_ve.get(half)
                    if prev:
                        ctx.h.wait_ge(sems["sc"], prev)
                for j in range(jmax):
                    kc = q4_ * 4 + j
                    i = ctx.h.transpose(ps_t3[:, half * 512 + j * 128: half * 512 + (j + 1) * 128],
                                        e1[:, kc * 128:(kc + 1) * 128], ident[:]) if ctx.is_("tensor") else None
                    pe = ctx.tick("pe", 1, i)
                ctx.wait("scalar", "pe", pe)
                dst = dT4[:, q4_ * 4 * REP3 * 128: (q4_ * 4 + jmax) * REP3 * 128].rearrange(
                    "p (a w) -> p a w", w=REP3 * 128)[:, :, h4 * 128:(h4 + 1) * 128]
                srcb = ps_t3[:, half * 512: half * 512 + jmax * 128].rearrange("p (a w) -> p a w", w=128)
                i = ctx.h.copy(dst, srcb) if ctx.is_("scalar") else None
                last = ctx.tick("sc", 1, i)
                dcopy_ve[half] = last
                dcopy_ve["last"] = last
            tpose_pe[par % 3] = ctx.counts["pe"]
            dcopy_ve[h4] = last

        W4 = REP3 * 128

        def emit_q_prefetch(ctx, g, pos):
            nonlocal_it = it_box[0]
            if ctx.is_("sync") and nonlocal_it >= 2:
                ctx.h.wait_ge(sems["pe"], qslot_pe[nonlocal_it - 2])
            qsem = f"dq{nonlocal_it % 2}"
            dq = None
            for h4 in range(REP3):
                h = g * REP3 + h4
                i = ctx.h.dma_start(
                    q4b[nonlocal_it % 2][:, h4 * 256:(h4 + 1) * 256].rearrange("p (a w) -> p a w", a=2),
                    q_scr.ap()[2 * h:2 * h + 2, :, pos * 128:(pos + 1) * 128].rearrange("a p w -> p a w"),
                ) if ctx.is_("sync") else None
                dq = ctx.tick(qsem, 16, i)
            return dq

        def emit_av_evac(ctx, g, pos, itv, dvv, vbuf):
            nonlocal av_done_sc, prev_stat_ve
            EC = 2 * pos + 2
            ctx.wait("tensor", "sc", dcopy_ve["last"])
            if pos == 0:
                ctx.wait("tensor", "dv3", dvv)
            if ctx.is_("tensor") and av_done_sc:
                ctx.h.wait_ge(sems["sc"], av_done_sc)
            for dvh in range(2):
                for kc in range(EC):
                    i = ctx.h.matmul(ps_av[:, dvh * 512: dvh * 512 + W4],
                                     vbuf[:, kc * 256 + dvh * 128: kc * 256 + (dvh + 1) * 128],
                                     dT4[:, kc * W4: (kc + 1) * W4],
                                     start=(kc == 0), stop=(kc == EC - 1)) if ctx.is_("tensor") else None
                    pe = ctx.tick("pe", 1, i)
            ctx.wait("scalar", "pe", pe)
            if itv >= 2:
                ctx.wait("scalar", f"soa{itv % 2}", aout[itv - 2])
                ctx.wait("scalar", "ve", astat_ve[itv - 2])
                ctx.wait("scalar", "gp", gp_last[0])
            asb = astage[itv % 2]
            i = ctx.h.copy(asb[:].rearrange("p (a w) -> p a w", w=W4),
                           ps_av[:].rearrange("p (a w) -> p a w", w=512)[:, :, 0:W4]) if ctx.is_("scalar") else None
            av_done_sc = ctx.tick("sc", 1, i)
            ctx.wait("sync", "sc", av_done_sc)
            for h4 in range(REP3):
                h = g * REP3 + h4
                i = ctx.h.dma_start(
                    att_scr.ap()[h, pos, :, :, :].rearrange("a p w -> p a w"),
                    asb[:].rearrange("p (a b w) -> p a b w", a=2, b=REP3)[:, :, h4, :],
                ) if ctx.is_("sync") else None
                do = ctx.tick(f"soa{itv % 2}", 16, i)
            aout[itv] = do
            ctx.wait("gpsimd", "sc", av_done_sc)
            ctx.wait("vector", "sc", av_done_sc)
            for h4 in range(REP3):
                h = g * REP3 + h4
                aview = asb[:].rearrange("p (a b w) -> p a b w", a=2, b=REP3)[:, :, h4, :]
                ctx.wait("gpsimd", "ve", prev_stat_ve)
                i = ctx.h.tensor_mul(sq3[:].rearrange("p (a w) -> p a w", a=2), aview, aview) if ctx.is_("gpsimd") else None
                gp = ctx.tick("gp", 1, i)
                gp_last[0] = gp
                ctx.wait("vector", "ve", prev_stat_ve)
                i = ctx.h.reduce_sum(stat_tmp[:, 0:1], aview, axis=AX.XY) if ctx.is_("vector") else None
                v = ctx.tick("ve", 1, i)
                ctx.wait("vector", "gp", gp)
                i = ctx.h.reduce_sum(stat_tmp[:, 1:2], sq3[:].rearrange("p (a w) -> p a w", a=2),
                                     axis=AX.XY) if ctx.is_("vector") else None
                v = ctx.tick("ve", 1, i)
                ctx.wait("vector", "ve", v)
                i = ctx.h.tensor_add(statp_s[:, h:h + 1], statp_s[:, h:h + 1], stat_tmp[:, 0:1]) if ctx.is_("vector") else None
                ctx.tick("ve", 1, i)
                i = ctx.h.tensor_add(statp_q[:, h:h + 1], statp_q[:, h:h + 1], stat_tmp[:, 1:2]) if ctx.is_("vector") else None
                v = ctx.tick("ve", 1, i)
                prev_stat_ve = v
            astat_ve[itv] = v

        it_box = [0]

        def load_vg(ctx, g):
            i = ctx.h.dma_start(
                v_g[g % 2][:].rearrange("p (c w) -> p c w", c=NT),
                v_scr.ap()[:, :, g * 256:(g + 1) * 256].rearrange("c p w -> p c w"),
            ) if ctx.is_("sync") else None
            return ctx.tick("dv3", 16, i)

        dvv = load_vg(ctx, 0)
        hctr_base = [0]
        for g in range(KV):
            vbuf = v_g[g % 2]
            stream = [(pos, h4) for pos in range(NO) for h4 in range(REP3)]
            dq_by_pos = {}
            it0 = it_box[0]

            def emit_T_for(idx):
                pos_, h4_ = stream[idx]
                hc = hctr_base[0] + idx
                emit_transposes(ctx, g, pos_, h4_, hc, 2 * pos_ + 2)
                if h4_ == REP3 - 1:
                    itv = it0 + pos_
                    qslot_pe[itv] = ctx.counts["pe"]
                    emit_av_evac(ctx, g, pos_, itv, dvv, vbuf)

            for idx, (pos, h4) in enumerate(stream):
                it_box[0] = it0 + pos
                if h4 == 0:
                    dq_by_pos[pos] = emit_q_prefetch(ctx, g, pos)
                    if pos == NO - 1 and g + 1 < KV:
                        dvv_next[0] = load_vg(ctx, g + 1)
                hc = hctr_base[0] + idx
                EC = 2 * pos + 2
                emit_scores(ctx, g, pos, h4, hc, dq_by_pos[pos], EC, it0 + pos)
                emit_combine(ctx, g, pos, h4, hc, EC)
                if idx >= 2:
                    emit_T_for(idx - 2)
            emit_T_for(len(stream) - 2)
            emit_T_for(len(stream) - 1)
            it_box[0] = it0 + NO
            hctr_base[0] += len(stream)
            if g + 1 < KV:
                dvv = dvv_next[0]

    phase("p3", p3)
    if stop_after == "p3":
        _finish_early(nc, sems, counts)
        return nc
    st3.close()
    stk.close()

    st4 = contextlib.ExitStack()
    srow = sb("srow", (1, 4 * H), st=st4)
    brow = sb("brow", (128, 4 * H), st=st4)
    ps_st = ps("ps_st", (1, 2 * H), st=st4)
    ps_bc = ps("ps_bc", (128, 4 * H), st=st4)
    NTOT = float(T * 256)

    def p4(ctx):
        i = ctx.h.matmul(ps_st[:, 0:H], onesc[:], statp_s[:], start=True, stop=True) if ctx.is_("tensor") else None
        ctx.tick("pe", 1, i)
        i = ctx.h.matmul(ps_st[:, H:2 * H], onesc[:], statp_q[:], start=True, stop=True) if ctx.is_("tensor") else None
        pe = ctx.tick("pe", 1, i)
        ctx.wait("vector", "pe", pe)
        i = ctx.h.tensor_copy(srow[:, 0:2 * H], ps_st[:]) if ctx.is_("vector") else None
        v = ctx.tick("ve", 1, i)
        ctx.wait("sync", "ve", v)
        i = ctx.h.dma_start(cc_in.ap(), srow[:, 0:2 * H]) if ctx.is_("sync") else None
        do = ctx.tick("socc", 16, i)
        ctx.wait("gpsimd", "socc", do)
        i = ctx.h.collective_compute(
            "AllReduce", ALU.add, replica_groups=[[2 * i_, 2 * i_ + 1] for i_ in range(cfg["n_cores"] // 2)],
            ins=[cc_in.ap()], outs=[cc_out.ap()],
        ) if ctx.is_("gpsimd") else None
        cc = ctx.tick("cc", 1, i)
        ctx.wait("sync", "cc", cc)
        i = ctx.h.dma_start(srow[:, 0:2 * H], cc_out.ap()) if ctx.is_("sync") else None
        dv = ctx.tick("dcc", 16, i)
        ctx.wait("vector", "dcc", dv)
        i = ctx.h.tensor_single_scalar(srow[:, 2 * H:3 * H], srow[:, 0:H], -1.0 / NTOT, ALU.mult) if ctx.is_("vector") else None
        v = ctx.tick("ve", 1, i)
        ctx.wait("vector", "ve", v)
        i = ctx.h.tensor_single_scalar(srow[:, H:2 * H], srow[:, H:2 * H], 1.0 / NTOT, ALU.mult) if ctx.is_("vector") else None
        v = ctx.tick("ve", 1, i)
        ctx.wait("vector", "ve", v)
        i = ctx.h.tensor_mul(srow[:, 3 * H:4 * H], srow[:, 2 * H:3 * H], srow[:, 2 * H:3 * H]) if ctx.is_("vector") else None
        v = ctx.tick("ve", 1, i)
        ctx.wait("vector", "ve", v)
        i = ctx.h.tensor_sub(srow[:, H:2 * H], srow[:, H:2 * H], srow[:, 3 * H:4 * H]) if ctx.is_("vector") else None
        v = ctx.tick("ve", 1, i)
        ctx.wait("vector", "ve", v)
        i = ctx.h.tensor_single_scalar(srow[:, H:2 * H], srow[:, H:2 * H], float(EPS), ALU.add) if ctx.is_("vector") else None
        v = ctx.tick("ve", 1, i)
        ctx.wait("vector", "ve", v)
        ctx.wait("scalar", "ve", v)
        i = ctx.h.activation(srow[:, 3 * H:4 * H], srow[:, H:2 * H], AF.Ln) if ctx.is_("scalar") else None
        s = ctx.tick("sc", 1, i)
        ctx.wait("scalar", "sc", s)
        i = ctx.h.activation(srow[:, 0:H], srow[:, 3 * H:4 * H], AF.Exp, scale=-0.5) if ctx.is_("scalar") else None
        s = ctx.tick("sc", 1, i)
        ctx.wait("tensor", "sc", s)
        i = ctx.h.matmul(ps_bc[:, 0:H], onesr[:], srow[:, 0:H], start=True, stop=True) if ctx.is_("tensor") else None
        ctx.tick("pe", 1, i)
        i = ctx.h.matmul(ps_bc[:, H:2 * H], onesr[:], srow[:, 2 * H:3 * H], start=True, stop=True) if ctx.is_("tensor") else None
        pe = ctx.tick("pe", 1, i)
        ctx.wait("vector", "pe", pe)
        i = ctx.h.tensor_copy(brow[:, 0:2 * H], ps_bc[:, 0:2 * H]) if ctx.is_("vector") else None
        v = ctx.tick("ve", 1, i)
        ctx.wait("vector", "ve", v)
        for dvh in range(2):
            i = ctx.h.tensor_mul(scale_vec[:, dvh * H:(dvh + 1) * H], gnw_sb[:, dvh * H:(dvh + 1) * H],
                                 brow[:, 0:H]) if ctx.is_("vector") else None
            v = ctx.tick("ve", 1, i)
        ctx.wait("vector", "ve", v)
        i = ctx.h.tensor_single_scalar(scale_vec[:], scale_vec[:], float(OUT_FACTOR), ALU.mult) if ctx.is_("vector") else None
        v = ctx.tick("ve", 1, i)
        ctx.wait("vector", "ve", v)
        for dvh in range(2):
            i = ctx.h.tensor_mul(bias_vec[:, dvh * H:(dvh + 1) * H], scale_vec[:, dvh * H:(dvh + 1) * H],
                                 brow[:, H:2 * H]) if ctx.is_("vector") else None
            v = ctx.tick("ve", 1, i)
        ctx.wait("vector", "ve", v)
        i = ctx.h.scalar_tensor_tensor(bias_vec[:], gnb_sb[:], float(OUT_FACTOR), bias_vec[:],
                                       ALU.mult, ALU.add) if ctx.is_("vector") else None
        ctx.tick("ve", 1, i)

    phase("p4", p4)
    if stop_after == "p4":
        _finish_early(nc, sems, counts)
        return nc
    st4.close()

    st5 = contextlib.ExitStack()
    TG = min(512, TOWN)
    NTG = TOWN // TG
    GT5 = TG // 128
    out_acc = sb("out_acc", (128, DC * TG), st=st5)
    yh = [sb("yh0", (128, 4 * TG), st=st5), sb("yh1", (128, 4 * TG), st=st5)]
    wos = [sb("wosA", (128, 2 * D), st=st5), sb("wosB", (128, 2 * D), st=st5)]
    x1g = sb("x1g", (128, 2 * D), st=st5)
    t5 = make_ffn_tiles(st5, 2)
    HP = (H + 1) // 2

    def p56(ctx):
        ffn_reset(t5)
        wo_used = {}
        wo_sub_used = {}
        ctx.wait("sync", "soa0", ctx.counts["soa0"])
        ctx.wait("sync", "soa1", ctx.counts["soa1"])
        ctx.wait("sync", "sox10", ctx.counts["sox10"])
        ctx.wait("sync", "sox11", ctx.counts["sox11"])
        prev_ve = prev_sc = 0
        ostore = {}
        sgi = 0
        for tg in range(NTG):
            pos0 = tg * GT5
            ctx.wait("vector", "pe", ctx.counts["pe"])
            i = ctx.h.memset(out_acc[:], 0.0) if ctx.is_("vector") else None
            v_acc = ctx.tick("ve", 1, i)
            for hp in range(HP):
                h0 = 2 * hp
                nheads = min(2, H - h0)
                nch = 2 * nheads
                ybuf = yh[hp % 2]
                prev = (tg, hp - 2) if hp >= 2 else ((tg - 1, HP - 2 + hp) if tg > 0 else None)
                if prev is not None and prev in wo_used:
                    ctx.wait("sync", "pe", wo_used[prev])
                ysem = f"dyh{hp % 2}"
                for j in range(nch):
                    h_, dvh = h0 + j // 2, j % 2
                    i = ctx.h.dma_start(
                        ybuf[:, j * TG:(j + 1) * TG].rearrange("p (o w) -> p o w", o=GT5),
                        att_scr.ap()[h_, pos0:pos0 + GT5, dvh, :, :].rearrange("o p w -> p o w"),
                    ) if ctx.is_("sync") else None
                    dv = ctx.tick(ysem, 16, i)
                ctx.wait("vector", ysem, dv)
                for j in range(nch):
                    h_ = h0 + j // 2
                    dvh = j % 2
                    col = dvh * H + h_
                    i = ctx.h.tensor_scalar(ybuf[:, j * TG:(j + 1) * TG], ybuf[:, j * TG:(j + 1) * TG],
                                            scale_vec[:, col:col + 1], bias_vec[:, col:col + 1],
                                            ALU.mult, ALU.add) if ctx.is_("vector") else None
                    v_n = ctx.tick("ve", 1, i)
                ctx.wait("tensor", "ve", v_n)
                ctx.wait("tensor", "ve", v_acc)
                psup = t5["ps_up"]
                slotguard = t5["state"]["psup_ve"]
                for sub in range(nch // 2):
                    sidx = hp * 2 + sub
                    sbuf_w = wos[sidx % 2]
                    prevu = wo_sub_used.get(sidx - 2)
                    if ctx.is_("sync") and prevu:
                        ctx.h.wait_ge(sems["pe"], prevu)
                    i = ctx.h.dma_start(sbuf_w[:], wo_st.ap()[:, (4 * hp + 2 * sub) * D:(4 * hp + 2 * sub + 2) * D]) if ctx.is_("sync") else None
                    dw_ = ctx.tick(f"dwt{sidx % 2}", 16, i)
                    ctx.wait("tensor", f"dwt{sidx % 2}", dw_)
                    for m in range(DC):
                        slot = m % 2
                        pslot = psup[:, slot * 512:slot * 512 + TG]
                        if ctx.is_("tensor") and slot in slotguard:
                            ctx.h.wait_ge(sems["ve"], slotguard[slot])
                        if ctx.is_("tensor") and slot in t5["state"].get("psup_sc", {}):
                            ctx.h.wait_ge(sems["sc"], t5["state"]["psup_sc"][slot])
                        for jl in range(2):
                            j = sub * 2 + jl
                            i = ctx.h.matmul(pslot, sbuf_w[:, jl * D + m * 128: jl * D + (m + 1) * 128],
                                             ybuf[:, j * TG:(j + 1) * TG],
                                             start=(jl == 0), stop=(jl == 1)) if ctx.is_("tensor") else None
                            pe = ctx.tick("pe", 1, i)
                        wo_sub_used[sidx] = pe
                        wo_used[(tg, hp)] = pe
                        ctx.wait("vector", "pe", pe)
                        i = ctx.h.tensor_add(out_acc[:, m * TG:(m + 1) * TG], out_acc[:, m * TG:(m + 1) * TG],
                                             pslot) if ctx.is_("vector") else None
                        slotguard[slot] = ctx.tick("ve", 1, i)
            ctx.wait("tensor", "ve", ctx.counts["ve"])
            for sg in range(GT5 // 2):
                ti0 = sg * 2
                c0 = pos0 + ti0
                if sgi > 0:
                    ctx.wait("sync", "ve", prev_ve)
                    ctx.wait("sync", "sc", prev_sc)
                    ctx.wait("scalar", "ve", prev_ve)
                    if (sgi - 1) in ostore:
                        ctx.wait("vector", "soo", ostore[sgi - 1])
                        ctx.wait("scalar", "soo", ostore[sgi - 1])
                i = ctx.h.dma_start(
                    x1g[:].rearrange("p (a w) -> p a w", a=2),
                    x1own_scr.ap()[c0:c0 + 2, :, :].rearrange("a p w -> p a w"),
                ) if ctx.is_("sync") else None
                dx1 = ctx.tick("dx1t", 16, i)
                X4v = t5["X4"][:].rearrange("p (a w) -> p a w", a=2)
                x1v = x1g[:].rearrange("p (a w) -> p a w", a=2)
                for m2 in range(DC // 2):
                    m = 2 * m2
                    ctx.wait("tensor", "ve", t5["state"]["pst_ve"])
                    for j in range(4):
                        ttl, mloc = j // 2, j % 2
                        i = ctx.h.transpose(t5["ps_t"][:, j * 128:(j + 1) * 128],
                                            out_acc[:, (m + mloc) * TG + (ti0 + ttl) * 128:
                                                    (m + mloc) * TG + (ti0 + ttl) * 128 + 128],
                                            ident[:]) if ctx.is_("tensor") else None
                        pe = ctx.tick("pe", 1, i)
                    ctx.wait("vector", "pe", pe)
                    ctx.wait("vector", "dx1t", dx1)
                    i = ctx.h.scalar_tensor_tensor(
                        X4v[:, :, m * 128:(m + 2) * 128],
                        t5["ps_t"][:].rearrange("p (a w) -> p a w", a=2), 1.0,
                        x1v[:, :, m * 128:(m + 2) * 128],
                        ALU.mult, ALU.add) if ctx.is_("vector") else None
                    v_x2 = ctx.tick("ve", 1, i)
                    t5["state"]["pst_ve"] = v_x2
                ctx.wait("vector", "pe", ctx.counts["pe"])
                def ydst5(tt, yb, y_ve):
                    emit_y_transposes(ctx, t5, tt, yb, y_ve, t5["YT4"][:], 256)
                emit_ln_group(ctx, t5, t5["X4"][:], t5["XR4"][:], ydst5, 2, src_ready=("ve", v_x2))
                v_o = emit_ffn_group(ctx, t5, 2)
                ctx.wait("sync", "ve", v_o)
                i = ctx.h.dma_start(
                    out_own.ap()[c0 * 128:(c0 + 2) * 128, :].rearrange("(a q) w -> q a w", a=2),
                    t5["XR4"][:].rearrange("p (a w) -> p a w", a=2),
                ) if ctx.is_("sync") else None
                ostore[sgi] = ctx.tick("soo", 16, i)
                prev_ve = ctx.counts["ve"]
                prev_sc = ctx.counts["sc"]
                sgi += 1
        ctx.wait("sync", "soo", ctx.counts["soo"])

    phase("p56", p56)
    st5.close()

    top.close()
    return nc



def kernel(**inputs):
    cfg = make_cfg()
    in_maps = make_core_inputs(cfg, inputs)
    nc = build_program(cfg)
    res = run_bass_kernel_spmd(nc, in_maps, core_ids=list(range(cfg["n_cores"])))
    return gather_output(cfg, res.results)
